# version 11
# baseline (speedup 1.0000x reference)
import os
from contextlib import ExitStack

import numpy as np
import ml_dtypes
import concourse.bass as bass
import concourse.mybir as mybir
from concourse.bass_utils import run_bass_kernel_spmd
from concourse.alu_op_type import AluOpType

F32 = mybir.dt.float32
BF16 = mybir.dt.bfloat16
U32 = mybir.dt.uint32
AF = mybir.ActivationFunctionType
AX = mybir.AxisListType
IOA = bass.IndirectOffsetOnAxis
OP = AluOpType

T = 4096
D = 1024
NT = T // 128
LN_EPS = 1e-5
ENGINES = ["sync", "tensor", "vector", "scalar", "gpsimd"]


class Ctx:
    def __init__(self, nc, block, stack):
        self.nc = nc
        self.block = block
        self.stack = stack
        self._sems = {}
        self.ctr = {}
        self.base = {}

    def rebase(self):
        self.base = dict(self.ctr)

    def w(self, eng, name, v):
        eng.wait_ge(self.sem(name), self.base.get(name, 0) + v)

    def sem(self, name):
        if name not in self._sems:
            self._sems[name] = self.stack.enter_context(self.nc.semaphore(name))
            self.ctr[name] = 0
        return self._sems[name]

    def inc(self, name, inst, n=1):
        inst.then_inc(self.sem(name), n)
        self.ctr[name] += n
        return self.ctr[name]

    def val(self, name):
        self.sem(name)
        return self.ctr[name]


class DChan:

    def __init__(self, ctx, base, nslots=2, per=16):
        self.c = ctx
        self.base = base
        self.nslots = nslots
        self.per = per

    def name(self, i):
        return f"{self.base}{i % self.nslots}"

    def issue(self, i, inst, n=16):
        self.c.inc(self.name(i), inst, n)

    def wait(self, eng, i):
        self.c.w(eng, self.name(i), self.per * (i // self.nslots + 1))

    def wait_all(self, eng):
        for s in range(self.nslots):
            n = f"{self.base}{s}"
            eng.wait_ge(self.c.sem(n), self.c.val(n))


def barrier(ctx, name, engines=None):
    if engines is None:
        engines = ["sync", "tensor", "vector", "scalar"]
    sem = ctx.sem(name)
    tgt = ctx.ctr[name] + len(engines)
    for en in engines:
        @getattr(ctx.block, en)
        def _(e):
            e.drain()
            e.sem_inc(sem, 1)
            e.wait_ge(sem, tgt)
    ctx.ctr[name] = tgt


def emit_ln_tile(c, i, xt, xn_out, stat, sq, vst, sst, xnr, eps_col,
                 waits_v=(), waits_s=()):
    block = c.block

    @block.vector
    def _(v):
        for f in waits_v:
            f(v)
        v.drain()
        v.tensor_reduce(stat[:, 0:1], xt, AX.X, OP.add)
        v.scalar_tensor_tensor(sq[:], xt, 1.0, xt, OP.mult, OP.mult,
                               accum_out=stat[:, 3:4])
        v.drain()
        v.tensor_scalar(stat[:, 2:3], stat[:, 0:1], stat[:, 0:1],
                        1.0 / (D * D), OP.mult, OP.mult)
        v.drain()
        c.inc(vst, v.tensor_scalar(stat[:, 4:5], stat[:, 3:4], 1.0 / D,
                                   stat[:, 2:3], OP.mult, OP.subtract))
        c.w(v, sst, i + 1)
        v.reciprocal(stat[:, 6:7], stat[:, 5:6])
        v.drain()
        c.inc(vst, v.tensor_scalar(stat[:, 7:8], stat[:, 0:1], stat[:, 6:7],
                                   -1.0 / D, OP.mult, OP.mult))
        v.drain()
        c.inc(xnr, v.tensor_scalar(xn_out, xt, stat[:, 6:7], stat[:, 7:8],
                                   OP.mult, OP.add))

    @block.scalar
    def _(s):
        for f in waits_s:
            f(s)
        c.w(s, vst, 2 * i + 1)
        c.inc(sst, s.activation(stat[:, 5:6], stat[:, 4:5], AF.Sqrt,
                                bias=eps_col[:], scale=1.0))


def emit_l0f(c, tag, x_in, x_out, xn_hbm, wq16_hbm, wq_gate, keys16, w16cat,
             cat_gate, keys_gate, iota_sb, ident_sb, eps_col, stages="ABC"):
    c.rebase()
    nc, block = c.nc, c.block
    vst, sst, xnr, trp, evc = [f"{tag}{n}" for n in
                               ["vst", "sst", "xnr", "trp", "evc"]]
    qm, qe, sm, ax, sce = [f"{tag}{n}" for n in ["qm", "qe", "sm", "ax", "sce"]]
    dt, ge, cr, sus, ua = [f"{tag}{n}" for n in ["dt", "ge", "cr", "sus", "ua"]]
    LD = DChan(c, f"{tag}ld", 2, 16)
    XN = DChan(c, f"{tag}xn", 2, 16)
    WL = DChan(c, f"{tag}wl", 2, 128)
    CL = DChan(c, f"{tag}cl", 2, 32)
    GD = DChan(c, f"{tag}gd", 8, 16)
    XO = DChan(c, f"{tag}xo", 2, 16)

    NPHI = int(os.environ.get("NPHI", "16"))
    PARTS = os.environ.get("PARTS", "wqsame")

    with ExitStack() as pstack:
        sbp = lambda n, s, d: pstack.enter_context(nc.sbuf_tensor(f"{tag}{n}", s, d, allow_name_mangling=True))
        mxa = sbp("mxa", [128, NT * 16], F32)
        ixa = sbp("ixa", [128, NT * 16], F32)
        scr = sbp("scr", [128, NT * 8], F32)
        idxu = sbp("idxu", [128, NT * 8], U32)
        abstack = ExitStack()
        sbab = lambda n, s, d: abstack.enter_context(nc.sbuf_tensor(f"{tag}{n}", s, d, allow_name_mangling=True))
        xnT = sbab("xnT", [128, 8 * T], BF16)
        keysT_sb = sbab("keysT", [128, 16 * 4 * 256], BF16)
        ktl = f"{tag}ktl"

        with ExitStack() as pa:
            sba = lambda n, s, d: pa.enter_context(nc.sbuf_tensor(f"{tag}{n}", s, d, allow_name_mangling=True))
            xs = sba("xs", [128, 2 * D], F32)
            xn16 = sba("xn16", [128, 2 * D], BF16)
            stat = sba("stat", [128, 16], F32)
            sq = sba("sq", [128, D], F32)
            psT = pa.enter_context(nc.psum_tensor(f"{tag}psT", [128, 1024], BF16, allow_name_mangling=True))

            for i in range(NT):
                bi = i % 2
                xt = xs[:, bi * D:(bi + 1) * D]
                xnt = xn16[:, bi * D:(bi + 1) * D]

                @block.sync
                def _(s, i=i, xt=xt):
                    if i >= 2:
                        c.w(s, xnr, i - 1)
                        c.w(s, trp, 8 * (i - 1))
                    LD.issue(i, s.dma_start(out=xt,
                                            in_=x_in[i * 128:(i + 1) * 128, :]))

                wv = [lambda e, i=i: LD.wait(e, i)]
                if i >= 2:
                    wv.append(lambda e, i=i: XN.wait(e, i - 2))
                emit_ln_tile(c, i, xt, xnt, stat, sq, vst, sst, xnr, eps_col,
                             waits_v=wv, waits_s=[lambda e, i=i: LD.wait(e, i)])

                @block.tensor
                def _(t, i=i, bi=bi):
                    c.w(t, xnr, i + 1)
                    if i >= 1:
                        c.w(t, evc, 8 * i)
                    for g in range(8):
                        c.inc(trp, t.transpose(
                            psT[:, g * 128:(g + 1) * 128],
                            xn16[:, bi * D + g * 128: bi * D + (g + 1) * 128],
                            ident_sb[:]))

                @block.vector
                def _(v, i=i):
                    c.w(v, trp, 8 * (i + 1))
                    for g in range(8):
                        c.inc(evc, v.tensor_copy(
                            xnT[:, g * T + i * 128: g * T + (i + 1) * 128],
                            psT[:, g * 128:(g + 1) * 128]))

                @block.sync
                def _(s, i=i, bi=bi):
                    c.w(s, xnr, i + 1)
                    XN.issue(i, s.dma_start(out=xn_hbm[i * 128:(i + 1) * 128, :],
                                            in_=xn16[:, bi * D:(bi + 1) * D]))

            @block.sync
            def _(s):
                XN.wait_all(s)
                LD.wait_all(s)

            barrier(c, "bar")

        if "B" not in stages:
            return
        with ExitStack() as pb:
            sbb = lambda n, s, d: pb.enter_context(nc.sbuf_tensor(f"{tag}{n}", s, d, allow_name_mangling=True))
            wq16 = sbb("wq16", [128, 2 * 4096], BF16)
            q16 = sbb("q16", [128, 2048], BF16)
            eqs = sbb("eqs", [128, 4 * 256], F32)
            sct = sbb("sct", [128, NT * 8], F32)
            psq = pb.enter_context(nc.psum_tensor(f"{tag}psq", [128, 4 * 512], F32, allow_name_mangling=True))
            psm = pb.enter_context(nc.psum_tensor(f"{tag}psm", [128, 4 * 256], F32, allow_name_mangling=True))

            @block.scalar
            def _(s):
                s.wait_ge(c.sem("wkd"), keys_gate)
                for phi in range(16):
                    p, h = phi // 8, phi % 8
                    for ds in range(4):
                        srcv = keys16[h, :, p, ds * 128:(ds + 1) * 128]
                        c.inc(ktl, s.dma_start(
                            out=keysT_sb[:, (phi * 4 + ds) * 256:(phi * 4 + ds + 1) * 256],
                            in_=srcv, transpose=True), 16)

            NCH = 8
            for phi in range(NPHI):
                p, h = phi // 8, phi % 8
                bi = phi % 2

                @block.sync
                def _(s, phi=phi, p=p, h=h, bi=bi):
                    if phi == 0:
                        s.wait_ge(c.sem("wkd"), wq_gate)
                    if phi >= 2:
                        c.w(s, qm, (phi - 1) * NCH)
                    for cg in range(8):
                        inst = s.dma_start(
                            out=wq16[:, bi * 4096 + cg * 512: bi * 4096 + (cg + 1) * 512],
                            in_=wq16_hbm[cg * 128:(cg + 1) * 128,
                                         (p * 8 + h) * 512:(p * 8 + h + 1) * 512])
                        WL.issue(phi, inst)

                for ch in range(NCH):
                    if "q" not in PARTS:
                        continue

                    @block.tensor
                    def _(t, phi=phi, ch=ch, bi=bi):
                        WL.wait(t, phi)
                        n = phi * NCH + ch
                        if n >= 1:
                            c.w(t, qe, 4 * n)
                        for ds in range(4):
                            for cg in range(8):
                                inst = t.matmul(
                                    psq[:, ds * 512:(ds + 1) * 512],
                                    wq16[:, bi * 4096 + cg * 512 + ds * 128:
                                         bi * 4096 + cg * 512 + (ds + 1) * 128],
                                    xnT[:, cg * T + ch * 512: cg * T + (ch + 1) * 512],
                                    start=(cg == 0), stop=(cg == 7))
                        c.inc(qm, inst)

                    if "s" not in PARTS:
                        continue

                    @block.vector
                    def _(v, phi=phi, ch=ch):
                        n = phi * NCH + ch
                        c.w(v, qm, n + 1)
                        if n >= 1:
                            c.w(v, sm, 4 * n)
                            c.w(v, qe, 4 * n)
                        for ds in range(4):
                            c.inc(qe, v.tensor_copy(q16[:, ds * 512:(ds + 1) * 512],
                                                    psq[:, ds * 512:(ds + 1) * 512]))

                    if "a" not in PARTS:
                        continue
                    for ts in range(4):
                        @block.tensor
                        def _(t, phi=phi, ch=ch, ts=ts):
                            if phi == 0 and ch == 0 and ts == 0:
                                c.w(t, ktl, 16 * 64)
                            c.w(t, qe, 4 * (phi * NCH + ch) + 4)
                            n = (phi * NCH + ch) * 4 + ts
                            if n >= 4:
                                c.w(t, ax, n - 3)
                            for ds in range(4):
                                inst = t.matmul(
                                    psm[:, ts * 256:(ts + 1) * 256],
                                    q16[:, ds * 512 + ts * 128: ds * 512 + (ts + 1) * 128],
                                    keysT_sb[:, (phi * 4 + ds) * 256:(phi * 4 + ds + 1) * 256],
                                    start=(ds == 0), stop=(ds == 3))
                            c.inc(sm, inst)

                    if "m" not in PARTS:
                        continue

                    @block.vector
                    def _(v, phi=phi, ch=ch):
                        nch = phi * NCH + ch
                        c.w(v, sm, 4 * nch + 4)
                        if nch >= 1:
                            c.w(v, ax, 4 * nch)
                        for ts in range(4):
                            col = (ch * 4 + ts) * 16 + phi
                            v.tensor_reduce(mxa[:, col:col + 1],
                                            psm[:, ts * 256:(ts + 1) * 256],
                                            AX.X, OP.max)
                        v.drain()
                        for ts in range(4):
                            col = (ch * 4 + ts) * 16 + phi
                            c.inc(ax, v.scalar_tensor_tensor(
                                eqs[:, ts * 256:(ts + 1) * 256],
                                psm[:, ts * 256:(ts + 1) * 256],
                                mxa[:, col:col + 1], iota_sb[:],
                                OP.is_equal, OP.mult,
                                accum_out=ixa[:, col:col + 1]))

            if "e" in PARTS:
                @block.vector
                def _(v):
                    c.w(v, ax, NT * 16)
                    mx3 = mxa[:].rearrange("p (t f) -> p t f", f=16)
                    ix3 = ixa[:].rearrange("p (t f) -> p t f", f=16)
                    sc3 = scr[:].rearrange("p (t f) -> p t f", f=8)
                    st3 = sct[:].rearrange("p (t f) -> p t f", f=8)
                    id3 = idxu[:].rearrange("p (t f) -> p t f", f=8)
                    v.tensor_tensor(st3, mx3[:, :, 0:8], mx3[:, :, 8:16], OP.add)
                    v.drain()
                    v.tensor_scalar(sc3, st3, 0.0, 0.5, OP.max, OP.mult)
                    v.drain()
                    c.inc(sce, v.scalar_tensor_tensor(
                        id3, ix3[:, :, 0:8], 256.0, ix3[:, :, 8:16],
                        OP.mult, OP.add))

            barrier(c, "bar")

        abstack.close()
        if "C" not in stages:
            return
        with ExitStack() as pc:
            sbc = lambda n, s, d: pc.enter_context(nc.sbuf_tensor(f"{tag}{n}", s, d, allow_name_mangling=True))
            ddu = sbc("ddu", [128, 8 * 2 * D], BF16)
            dsb = sbc("dsb", [128, 4 * D], BF16)
            hds = sbc("hds", [128, 16], F32)
            cv = sbc("cv", [128, 8], F32)
            su = sbc("su", [128, 8 * D], BF16)
            ta = sbc("ta", [128, 4 * D], F32)
            tb = sbc("tb", [128, 2 * D], F32)
            tc = sbc("tc", [128, D], F32)
            accF = sbc("accF", [128, 2 * D], F32)
            xc = sbc("xc", [128, 2 * D], F32)
            xnc = sbc("xnc", [128, 2 * D], BF16)

            for i in range(NT):
                bi = i % 2

                @block.sync
                def _(s, i=i, bi=bi):
                    if i >= 2:
                        c.w(s, ua, i - 1)
                        c.w(s, dt, 8 * (i - 1))
                    CL.issue(i, s.dma_start(out=xc[:, bi * D:(bi + 1) * D],
                                            in_=x_in[i * 128:(i + 1) * 128, :]))
                    CL.issue(i, s.dma_start(out=xnc[:, bi * D:(bi + 1) * D],
                                            in_=xn_hbm[i * 128:(i + 1) * 128, :]))

                for h in range(8):
                    slot = h

                    @block.gpsimd
                    def _(g, i=i, h=h, slot=slot):
                        n = i * 8 + h
                        if n == 0:
                            c.w(g, sce, 1)
                            g.wait_ge(c.sem("wkd"), cat_gate)
                        if n >= 8:
                            c.w(g, sus, n - 7)
                        io = IOA(ap=idxu[:, n:n + 1], axis=0)
                        GD.issue(n, g.indirect_dma_start(
                            out=ddu[:, slot * 2 * D:(slot + 1) * 2 * D],
                            out_offset=None, in_=w16cat[:, :], in_offset=io))

                    @block.vector
                    def _(v, i=i, h=h, bi=bi, slot=slot):
                        n = i * 8 + h
                        if h == 0:
                            if i == 0:
                                c.w(v, sce, 1)
                            else:
                                c.w(v, dt, 8 * i)
                        if n >= 4:
                            c.w(v, dt, n - 3)
                        GD.wait(v, n)
                        CL.wait(v, i)
                        c.inc(dt, v.scalar_tensor_tensor(
                            dsb[:, (h % 4) * D:(h % 4 + 1) * D],
                            ddu[:, slot * 2 * D: slot * 2 * D + D], 1.0,
                            xnc[:, bi * D:(bi + 1) * D], OP.mult, OP.mult,
                            accum_out=hds[:, h:h + 1]))

                @block.scalar
                def _(s, i=i):
                    c.w(s, dt, 8 * (i + 1))
                    if i >= 1:
                        c.w(s, ge, i)
                    c.inc(ge, s.activation(hds[:, 8:16], hds[:, 0:8], AF.Gelu))

                @block.vector
                def _(v, i=i):
                    c.w(v, ge, i + 1)
                    if i >= 1:
                        c.w(v, cr, i)
                    c.inc(cr, v.tensor_tensor(cv[:], hds[:, 8:16],
                                              scr[:, i * 8:(i + 1) * 8], OP.mult))

                for h in range(8):
                    slot = h

                    @block.vector
                    def _(v, i=i, h=h, slot=slot):
                        n = i * 8 + h
                        c.w(v, cr, i + 1)
                        c.inc(sus, v.tensor_scalar(
                            su[:, h * D:(h + 1) * D],
                            ddu[:, slot * 2 * D + D: slot * 2 * D + 2 * D],
                            cv[:, h:h + 1], None, OP.mult, OP.bypass))

                @block.vector
                def _(v, i=i, bi=bi):
                    c.w(v, sus, 8 * (i + 1))
                    if i >= 1:
                        c.w(v, ua, i)
                    if i >= 2:
                        XO.wait(v, i - 2)
                    for k in range(4):
                        v.tensor_tensor(ta[:, k * D:(k + 1) * D],
                                        su[:, 2 * k * D:(2 * k + 1) * D],
                                        su[:, (2 * k + 1) * D:(2 * k + 2) * D],
                                        OP.add)
                    v.drain()
                    for k in range(2):
                        v.tensor_tensor(tb[:, k * D:(k + 1) * D],
                                        ta[:, 2 * k * D:(2 * k + 1) * D],
                                        ta[:, (2 * k + 1) * D:(2 * k + 2) * D],
                                        OP.add)
                    v.drain()
                    v.tensor_tensor(tc[:], tb[:, 0:D], tb[:, D:2 * D], OP.add)
                    v.drain()
                    c.inc(ua, v.tensor_tensor(accF[:, bi * D:(bi + 1) * D],
                                              tc[:], xc[:, bi * D:(bi + 1) * D],
                                              OP.add))

                @block.sync
                def _(s, i=i, bi=bi):
                    c.w(s, ua, i + 1)
                    XO.issue(i, s.dma_start(out=x_out[i * 128:(i + 1) * 128, :],
                                            in_=accF[:, bi * D:(bi + 1) * D]))

            @block.sync
            def _(s):
                XO.wait_all(s)
                CL.wait_all(s)
                GD.wait_all(s)

            barrier(c, "bar")


def emit_keys_cast(c, keys, keys16, kbuf=None):
    nc, block = c.nc, c.block
    sf = keys.rearrange("a b c d -> (a b c d)").rearrange("(m n) -> m n", n=2048)
    df = keys16.rearrange("a b c d -> (a b c d)").rearrange("(m n) -> m n", n=2048)

    @block.gpsimd
    def _(g):
        c.inc("wkc", g.dma_start(out=df[:, :], in_=sf[:, :]), 16)
        g.wait_ge(c.sem("wkc"), c.val("wkc"))
        g.sem_inc(c.sem("wkd"), 1)
        c.ctr["wkd"] = c.val("wkd") + 1
    return c.val("wkd")


def emit_cast16(c, tag, pairs, buf=None):
    nc, block = c.nc, c.block
    ckpts = []
    R = 4096

    @block.gpsimd
    def _(g):
        for src_, dst in pairs:
            n = 1
            for d in src_.shape:
                n *= d
            names = " ".join(f"a{i}" for i in range(len(src_.shape)))
            sf = src_.rearrange(f"{names} -> ({names})").rearrange(
                "(m n) -> m n", n=2048)
            df = dst.rearrange(f"{names} -> ({names})").rearrange(
                "(m n) -> m n", n=2048)
            m = n // 2048
            for r in range(0, m, R):
                w = min(R, m - r)
                c.inc("wkc", g.dma_start(out=df[r:r + w, :],
                                         in_=sf[r:r + w, :]), 16)
            g.wait_ge(c.sem("wkc"), c.val("wkc"))
            g.sem_inc(c.sem("wkd"), 1)
            c.ctr["wkd"] = c.val("wkd") + 1
            ckpts.append(c.val("wkd"))
    return ckpts


def emit_cat_cast(c, w_down, w_up, w16cat, buf=None):
    nc, block = c.nc, c.block
    R = 4096

    @block.gpsimd
    def _(g):
        for half, srcm in ((0, w_down), (1, w_up)):
            for r in range(0, 65536, R):
                c.inc("wkc", g.dma_start(
                    out=w16cat[r:r + R, half * 1024:(half + 1) * 1024],
                    in_=srcm[r:r + R, :]), 16)
        g.wait_ge(c.sem("wkc"), c.val("wkc"))
        g.sem_inc(c.sem("wkd"), 1)
        c.ctr["wkd"] = c.val("wkd") + 1
    return c.val("wkd")


def emit_ln_chunk(c, names, ci, x_in, xnTc, bufs, eps_col, ident_sb,
                  keep_x=None, tensor_waits0=()):
    block = c.block
    LD, vst, sst, xnr, trp, evc = (names["LD"], names["vst"], names["sst"],
                                   names["xnr"], names["trp"], names["evc"])
    xs, xn16, stat, sq, psT = bufs
    for it in range(4):
        i = ci * 4 + it
        bi = i % 2
        xt = xs[:, bi * D:(bi + 1) * D]
        xnt = xn16[:, bi * D:(bi + 1) * D]

        @block.sync
        def _(s, i=i, xt=xt):
            if i >= 2:
                c.w(s, xnr, i - 1)
                c.w(s, trp, 8 * (i - 1))
            LD.issue(i, s.dma_start(out=xt, in_=x_in[i * 128:(i + 1) * 128, :]))
            if keep_x is not None:
                LD.issue(i, s.dma_start(out=keep_x(i),
                                        in_=x_in[i * 128:(i + 1) * 128, :]))

        emit_ln_tile(c, i, xt, xnt, stat, sq, vst, sst, xnr, eps_col,
                     waits_v=[lambda e, i=i: LD.wait(e, i)],
                     waits_s=[lambda e, i=i: LD.wait(e, i)])

        @block.tensor
        def _(t, i=i, bi=bi):
            c.w(t, xnr, i + 1)
            if i == 0:
                for f in tensor_waits0:
                    f(t)
            if i >= 1:
                c.w(t, evc, 8 * i)
            for g in range(8):
                c.inc(trp, t.transpose(
                    psT[:, g * 128:(g + 1) * 128],
                    xn16[:, bi * D + g * 128: bi * D + (g + 1) * 128],
                    ident_sb[:]))

        @block.scalar
        def _(s, i=i, it=it):
            c.w(s, trp, 8 * (i + 1))
            for g in range(8):
                c.inc(evc, s.copy(xnTc[:, g * 512 + it * 128: g * 512 + (it + 1) * 128],
                                  psT[:, g * 128:(g + 1) * 128]))



def emit_l1f(c, tag, x_in, x_out, w_in16, w_out16, w1_16, w2_16, wg16, we16,
             gb, eb, hT_hbm, acc_hbm, xnT_hbm, ident_sb, eps_col, ones1_sb):
    c.rebase()
    nc, block = c.nc, c.block
    nm = lambda n: f"{tag}{n}"
    vst, sst, xnr, trp, evc = map(nm, ["vst", "sst", "xnr", "trp", "evc"])
    wtl, ym, sil, hmu, rm, rva, rvs, rv, shm, acs, ini = map(
        nm, ["wtl", "ym", "sil", "hmu", "rm", "rva", "rvs", "rv", "shm",
             "acs", "ini"])
    ym2, sil2, hm2 = map(nm, ["ym2", "sil2", "hm2"])
    bm, ba, res = map(nm, ["bm", "ba", "res"])
    LD = DChan(c, nm("ld"), 2, 16)
    SH = DChan(c, nm("sh"), 2, 16)
    HT = DChan(c, nm("ht"), 2, 256)
    XT = DChan(c, nm("xn"), 2, 128)
    HL = DChan(c, nm("hl"), 2, 256)
    AL = DChan(c, nm("cl"), 2, 32)
    AO = DChan(c, nm("xo"), 2, 16)

    with ExitStack() as pstack:
        sbp = lambda n, s, d: pstack.enter_context(nc.sbuf_tensor(f"{tag}{n}", s, d, allow_name_mangling=True))
        w_se = sbp("wse", [128, NT * 8], F32)

        with ExitStack() as pa:
            sba = lambda n, s, d: pa.enter_context(nc.sbuf_tensor(f"{tag}{n}", s, d, allow_name_mangling=True))
            xs = sba("xs", [128, 2 * D], F32)
            xn16 = sba("xn16", [128, 2 * D], BF16)
            stat = sba("stat", [128, 16], F32)
            sq = sba("sq", [128, D], F32)
            psT = pa.enter_context(nc.psum_tensor(f"{tag}psT", [128, 1024], BF16, allow_name_mangling=True))
            winT = sba("winT", [128, 8 * 4096], BF16)
            woutT = sba("woutT", [128, 16 * 1024], BF16)
            wgT = sba("wgT", [128, 8 * 2], BF16)
            weT = sba("weT", [128, 8 * 8], BF16)
            wgp = sba("wgp", [2, 1024], BF16)
            wep = sba("wep", [8, 1024], BF16)
            gbw = sba("gbw", [1, 2], BF16)
            ebw = sba("ebw", [1, 8], BF16)
            gbf = sba("gbf", [1, 2], F32)
            ebf = sba("ebf", [1, 8], F32)
            xnTc = sba("xnTc", [128, 2 * 4096], BF16)
            hsTc = sba("hsTc", [128, 16 * 512], BF16)
            sa16 = sba("sa16", [128, 2 * 512], BF16)
            sb16 = sba("sb16", [128, 2 * 512], BF16)
            accs = sba("accs", [128, 2 * D], F32)
            rsc = sba("rsc", [128, 34], F32)
            psy = pa.enter_context(nc.psum_tensor(f"{tag}psy", [128, 2 * 512], F32, allow_name_mangling=True))
            psyB = pa.enter_context(nc.psum_tensor(f"{tag}psyB", [128, 2 * 512], F32, allow_name_mangling=True))
            psr = pa.enter_context(nc.psum_tensor(f"{tag}psr", [128, 16], F32, allow_name_mangling=True))
            pss = pa.enter_context(nc.psum_tensor(f"{tag}pss", [128, 2 * 512], F32, allow_name_mangling=True))

            @block.sync
            def _(s):
                s.wait_ge(c.sem("wkd"), c.val("wkd"))
                for cg in range(8):
                    c.inc(wtl, s.dma_start(
                        out=winT[:, cg * 4096:(cg + 1) * 4096],
                        in_=w_in16[:, cg * 128:(cg + 1) * 128], transpose=True), 16)
                for jg in range(16):
                    c.inc(wtl, s.dma_start(
                        out=woutT[:, jg * 1024:(jg + 1) * 1024],
                        in_=w_out16[:, jg * 128:(jg + 1) * 128], transpose=True), 16)
                c.inc(wtl, s.dma_start(out=wgp[:], in_=wg16[:, :]), 16)
                c.inc(wtl, s.dma_start(out=wep[:], in_=we16[:, :]), 16)
                c.inc(wtl, s.dma_start(out=gbf[:], in_=gb.unsqueeze(0)), 16)
                c.inc(wtl, s.dma_start(out=ebf[:], in_=eb.unsqueeze(0)), 16)
                s.wait_ge(c.sem(wtl), c.val(wtl))

            @block.vector
            def _(v):
                v.wait_ge(c.sem(wtl), c.val(wtl))
                v.tensor_copy(gbw[:], gbf[:])
                v.tensor_copy(ebw[:], ebf[:])
                c.inc(ini, v.memset(ones1_sb[:], 1.0))

            rwt = nm("rwt")

            @block.tensor
            def _(t):
                t.wait_ge(c.sem(ini), c.val(ini))
                t.wait_ge(c.sem(wtl), c.val(wtl))
                for cg in range(8):
                    t.transpose(psT[:, cg * 2:(cg + 1) * 2],
                                wgp[:, cg * 128:(cg + 1) * 128],
                                ident_sb[0:2, 0:2])
                for cg in range(8):
                    inst = t.transpose(psT[:, 16 + cg * 8:16 + (cg + 1) * 8],
                                       wep[:, cg * 128:(cg + 1) * 128],
                                       ident_sb[0:8, 0:8])
                c.inc(rwt, inst)

            @block.scalar
            def _(s):
                c.w(s, rwt, 1)
                s.copy(wgT[:], psT[:, 0:16])
                c.inc(rwt, s.copy(weT[:], psT[:, 16:80]))

            for ci in range(8):
                cb = ci % 2
                xnT = xnTc[:, cb * 4096:(cb + 1) * 4096]

                @block.scalar
                def _(s, ci=ci):
                    if ci >= 2:
                        c.w(s, ym, 16 * (ci - 1))
                        c.w(s, rm, 4 * (ci - 1))
                        XT.wait(s, ci - 2)

                emit_ln_chunk(c, {"LD": LD, "vst": vst, "sst": sst, "xnr": xnr,
                                  "trp": trp, "evc": evc}, ci, x_in, xnT,
                              (xs, xn16, stat, sq, psT), eps_col, ident_sb,
                              tensor_waits0=[lambda t: c.w(t, rwt, 2)])

                @block.sync
                def _(s, ci=ci, xnT=xnT):
                    c.w(s, evc, 32 * (ci + 1))
                    for g in range(8):
                        XT.issue(ci, s.dma_start(
                            out=xnT_hbm[g, :, ci * 512:(ci + 1) * 512],
                            in_=xnT[:, g * 512:(g + 1) * 512]))

                for jp in range(16):
                    @block.tensor
                    def _(t, ci=ci, jp=jp, xnT=xnT):
                        c.w(t, evc, 32 * (ci + 1))
                        njp = ci * 16 + jp
                        if njp >= 2:
                            c.w(t, sil, njp - 1)
                            c.w(t, hmu, 2 * njp - 3)
                        if jp == 0 and ci >= 1:
                            c.w(t, shm, 4 * ci)
                        psyx = psy if njp % 2 == 0 else psyB
                        for half, jj in ((0, jp), (1, jp + 16)):
                            for cg in range(8):
                                inst = t.matmul(
                                    psyx[:, half * 512:(half + 1) * 512],
                                    winT[:, cg * 4096 + jj * 128: cg * 4096 + (jj + 1) * 128],
                                    xnT[:, cg * 512:(cg + 1) * 512],
                                    start=(cg == 0), stop=(cg == 7))
                        c.inc(ym, inst)

                    @block.scalar
                    def _(s, ci=ci, jp=jp):
                        njp = ci * 16 + jp
                        c.w(s, ym, njp + 1)
                        if njp >= 2:
                            c.w(s, hmu, 2 * njp - 2)
                            c.w(s, sil, njp - 1)
                        sl = njp % 2
                        psyx = psy if njp % 2 == 0 else psyB
                        c.inc(sil, s.activation(sa16[:, sl * 512:(sl + 1) * 512],
                                                psyx[:, 0:512], AF.Silu))

                    @block.vector
                    def _(v, ci=ci, jp=jp):
                        njp = ci * 16 + jp
                        c.w(v, sil, njp + 1)
                        if jp == 0 and ci >= 1:
                            c.w(v, hmu, 32 * ci)
                        if njp >= 2:
                            c.w(v, hmu, 2 * njp - 2)
                        sl = njp % 2
                        psyx = psy if njp % 2 == 0 else psyB
                        c.inc(hmu, v.tensor_copy(sb16[:, sl * 512:(sl + 1) * 512],
                                                 psyx[:, 512:1024]))
                        c.w(v, hmu, 2 * njp + 1)
                        c.inc(hmu, v.tensor_tensor(
                            hsTc[:, jp * 512:(jp + 1) * 512],
                            sa16[:, sl * 512:(sl + 1) * 512],
                            sb16[:, sl * 512:(sl + 1) * 512], OP.mult))

                for ts in range(4):
                    nt_ = ci * 4 + ts

                    @block.tensor
                    def _(t, ci=ci, ts=ts, nt_=nt_, xnT=xnT):
                        if nt_ >= 1:
                            c.w(t, rva, nt_)
                            c.w(t, rvs, nt_)
                        for cg in range(8):
                            t.matmul(psr[:, 0:2],
                                     xnT[:, cg * 512 + ts * 128: cg * 512 + (ts + 1) * 128],
                                     wgT[:, cg * 2:(cg + 1) * 2],
                                     start=(cg == 0), stop=False)
                        t.matmul(psr[:, 0:2], ones1_sb[:], gbw[:],
                                 start=False, stop=True)
                        for cg in range(8):
                            t.matmul(psr[:, 8:16],
                                     xnT[:, cg * 512 + ts * 128: cg * 512 + (ts + 1) * 128],
                                     weT[:, cg * 8:(cg + 1) * 8],
                                     start=(cg == 0), stop=False)
                        inst = t.matmul(psr[:, 8:16], ones1_sb[:], ebw[:],
                                        start=False, stop=True)
                        c.inc(rm, inst)

                    emit_router(c, nt_, psr, rsc, w_se, rm, rva, rvs, rv)

                    @block.tensor
                    def _(t, ci=ci, ts=ts, nt_=nt_):
                        c.w(t, hmu, 32 * (ci + 1))
                        if nt_ >= 1:
                            c.w(t, acs, nt_)
                        for dh in range(2):
                            for jg in range(16):
                                inst = t.matmul(
                                    pss[:, dh * 512:(dh + 1) * 512],
                                    hsTc[:, jg * 512 + ts * 128: jg * 512 + (ts + 1) * 128],
                                    woutT[:, jg * 1024 + dh * 512: jg * 1024 + (dh + 1) * 512],
                                    start=(jg == 0), stop=(jg == 15))
                        c.inc(shm, inst)

                    @block.vector
                    def _(v, nt_=nt_):
                        c.w(v, shm, nt_ + 1)
                        if nt_ >= 1:
                            c.w(v, acs, nt_)
                        if nt_ >= 2:
                            SH.wait(v, nt_ - 2)
                        sl = nt_ % 2
                        c.inc(acs, v.tensor_scalar(accs[:, sl * D:(sl + 1) * D],
                                                   pss[:], 0.5, None,
                                                   OP.mult, OP.bypass))

                    @block.sync
                    def _(s, nt_=nt_):
                        c.w(s, acs, nt_ + 1)
                        sl = nt_ % 2
                        SH.issue(nt_, s.dma_start(
                            out=acc_hbm[nt_ * 128:(nt_ + 1) * 128, :],
                            in_=accs[:, sl * D:(sl + 1) * D]))

            @block.sync
            def _(s):
                SH.wait_all(s)
                XT.wait_all(s)
                LD.wait_all(s)

            barrier(c, nm("bA1"))

        with ExitStack() as pa2:
            sba = lambda n, s, d: pa2.enter_context(nc.sbuf_tensor(f"{tag}{n}", s, d, allow_name_mangling=True))
            w1T = sba("w1T", [128, 8 * 4096], BF16)
            xnTc2 = sba("xnTc2", [128, 2 * 4096], BF16)
            sa2 = sba("sa2", [128, 2 * 512], BF16)
            sb2 = sba("sb2", [128, 2 * 512], BF16)
            hTc = sba("hTc", [128, 2 * 16 * 512], BF16)
            psy2 = pa2.enter_context(nc.psum_tensor(f"{tag}psy2", [128, 2 * 512], F32, allow_name_mangling=True))
            psy2B = pa2.enter_context(nc.psum_tensor(f"{tag}psy2B", [128, 2 * 512], F32, allow_name_mangling=True))

            @block.sync
            def _(s):
                s.wait_ge(c.sem("wkd"), c.val("wkd"))
                for cg in range(8):
                    c.inc(wtl, s.dma_start(
                        out=w1T[:, cg * 4096:(cg + 1) * 4096],
                        in_=w1_16[:, cg * 128:(cg + 1) * 128], transpose=True), 16)
                s.wait_ge(c.sem(wtl), c.val(wtl))

            @block.tensor
            def _(t):
                t.wait_ge(c.sem(wtl), c.val(wtl))

            for ci in range(8):
                cb = ci % 2
                xnT = xnTc2[:, cb * 4096:(cb + 1) * 4096]

                @block.sync
                def _(s, ci=ci, xnT=xnT):
                    if ci >= 2:
                        c.w(s, ym2, 16 * (ci - 1))
                    for g in range(8):
                        XT.issue(8 + ci, s.dma_start(
                            out=xnT[:, g * 512:(g + 1) * 512],
                            in_=xnT_hbm[g, :, ci * 512:(ci + 1) * 512]))

                for jp in range(16):
                    @block.tensor
                    def _(t, ci=ci, jp=jp, xnT=xnT):
                        XT.wait(t, 8 + ci)
                        njp = ci * 16 + jp
                        if njp >= 2:
                            c.w(t, sil2, njp - 1)
                            c.w(t, hm2, 2 * njp - 3)
                        psyx = psy2 if njp % 2 == 0 else psy2B
                        for half, jj in ((0, jp), (1, jp + 16)):
                            for cg in range(8):
                                inst = t.matmul(
                                    psyx[:, half * 512:(half + 1) * 512],
                                    w1T[:, cg * 4096 + jj * 128: cg * 4096 + (jj + 1) * 128],
                                    xnT[:, cg * 512:(cg + 1) * 512],
                                    start=(cg == 0), stop=(cg == 7))
                        c.inc(ym2, inst)

                    @block.scalar
                    def _(s, ci=ci, jp=jp):
                        njp = ci * 16 + jp
                        c.w(s, ym2, njp + 1)
                        if njp >= 2:
                            c.w(s, hm2, 2 * njp - 2)
                            c.w(s, sil2, njp - 1)
                        sl = njp % 2
                        psyx = psy2 if njp % 2 == 0 else psy2B
                        c.inc(sil2, s.activation(sa2[:, sl * 512:(sl + 1) * 512],
                                                 psyx[:, 0:512], AF.Silu))

                    @block.vector
                    def _(v, ci=ci, jp=jp, cb=cb):
                        njp = ci * 16 + jp
                        c.w(v, sil2, njp + 1)
                        if jp == 0 and ci >= 2:
                            HT.wait(v, ci - 2)
                            c.w(v, hm2, 32 * (ci - 1))
                        if njp >= 2:
                            c.w(v, hm2, 2 * njp - 2)
                        sl = njp % 2
                        psyx = psy2 if njp % 2 == 0 else psy2B
                        c.inc(hm2, v.tensor_copy(sb2[:, sl * 512:(sl + 1) * 512],
                                                 psyx[:, 512:1024]))
                        c.w(v, hm2, 2 * njp + 1)
                        c.inc(hm2, v.tensor_tensor(
                            hTc[:, cb * 8192 + jp * 512: cb * 8192 + (jp + 1) * 512],
                            sa2[:, sl * 512:(sl + 1) * 512],
                            sb2[:, sl * 512:(sl + 1) * 512], OP.mult))

                @block.sync
                def _(s, ci=ci, cb=cb):
                    c.w(s, hm2, 32 * (ci + 1))
                    for jg in range(16):
                        HT.issue(ci, s.dma_start(
                            out=hT_hbm[jg, :, ci * 512:(ci + 1) * 512],
                            in_=hTc[:, cb * 8192 + jg * 512: cb * 8192 + (jg + 1) * 512]))

            @block.sync
            def _(s):
                HT.wait_all(s)
                XT.wait_all(s)

            barrier(c, nm("bA2"))

        for half in range(2):
            with ExitStack() as pb:
                sbb = lambda n, s, d: pb.enter_context(
                    nc.sbuf_tensor(f"{tag}{n}_{half}", s, d, allow_name_mangling=True))
                w2T = sbb("w2T", [128, 4 * 16 * 1024], BF16)
                hTt = sbb("hTt", [128, 2 * 16 * 512], BF16)
                acci = sbb("acci", [128, 2 * D], F32)
                acc = sbb("acc", [128, 2 * D], F32)
                xcb = sbb("xcb", [128, 2 * D], F32)
                psb = pb.enter_context(nc.psum_tensor(f"{tag}psb{half}",
                                                      [128, 2 * 1024], F32, allow_name_mangling=True))

                @block.sync
                def _(s, half=half):
                    s.wait_ge(c.sem("wkd"), c.val("wkd"))
                    for e in range(4):
                        for jg in range(16):
                            c.inc(wtl, s.dma_start(
                                out=w2T[:, (e * 16 + jg) * 1024:(e * 16 + jg + 1) * 1024],
                                in_=w2_16[half * 4 + e, :, jg * 128:(jg + 1) * 128],
                                transpose=True), 16)
                    s.wait_ge(c.sem(wtl), c.val(wtl))

                @block.tensor
                def _(t):
                    t.wait_ge(c.sem(wtl), c.val(wtl))

                for st in range(8):
                    sb_ = st % 2
                    gst = half * 8 + st

                    @block.sync
                    def _(s, st=st, sb_=sb_, gst=gst):
                        if gst >= 2:
                            c.w(s, bm, 16 * (gst - 1))
                        for jg in range(16):
                            HL.issue(gst, s.dma_start(
                                out=hTt[:, sb_ * 8192 + jg * 512: sb_ * 8192 + (jg + 1) * 512],
                                in_=hT_hbm[jg, :, st * 512:(st + 1) * 512]))

                    for tt in range(4):
                        i = st * 4 + tt
                        bi = i % 2
                        gidx = half * NT + i

                        @block.sync
                        def _(s, i=i, bi=bi, half=half, gidx=gidx):
                            if gidx >= 2:
                                c.w(s, ba, 4 * (gidx - 1))
                                c.w(s, res, gidx - 1)
                                AO.wait(s, gidx - 2)
                            AL.issue(gidx, s.dma_start(
                                out=acci[:, bi * D:(bi + 1) * D],
                                in_=acc_hbm[i * 128:(i + 1) * 128, :]))
                            if half == 1:
                                AL.issue(gidx, s.dma_start(
                                    out=xcb[:, bi * D:(bi + 1) * D],
                                    in_=x_in[i * 128:(i + 1) * 128, :]))
                            else:
                                AL.issue(gidx, s.dma_start(
                                    out=xcb[:, bi * D:(bi + 1) * D],
                                    in_=acc_hbm[i * 128:(i + 1) * 128, :]))

                        for e in range(4):
                            @block.tensor
                            def _(t, e=e, sb_=sb_, tt=tt, gidx=gidx, gst=gst):
                                ne = gidx * 4 + e
                                HL.wait(t, gst)
                                if ne >= 2:
                                    c.w(t, ba, ne - 1)
                                pb_ = psb[:, (e % 2) * 1024:(e % 2 + 1) * 1024]
                                for dh in range(2):
                                    for jg in range(16):
                                        inst = t.matmul(
                                            pb_[:, dh * 512:(dh + 1) * 512],
                                            hTt[:, sb_ * 8192 + jg * 512 + tt * 128:
                                                sb_ * 8192 + jg * 512 + (tt + 1) * 128],
                                            w2T[:, (e * 16 + jg) * 1024 + dh * 512:
                                                (e * 16 + jg) * 1024 + (dh + 1) * 512],
                                            start=(jg == 0), stop=(jg == 15))
                                c.inc(bm, inst)

                            @block.vector
                            def _(v, i=i, e=e, bi=bi, half=half, gidx=gidx):
                                ne = gidx * 4 + e
                                c.w(v, bm, ne + 1)
                                if ne >= 1:
                                    c.w(v, ba, ne)
                                AL.wait(v, gidx)
                                pb_ = psb[:, (e % 2) * 1024:(e % 2 + 1) * 1024]
                                src = (acci[:, bi * D:(bi + 1) * D] if e == 0
                                       else acc[:, ((e + 1) % 2) * D:((e + 1) % 2 + 1) * D])
                                eg = half * 4 + e
                                c.inc(ba, v.scalar_tensor_tensor(
                                    acc[:, (e % 2) * D:(e % 2 + 1) * D], pb_,
                                    w_se[:, i * 8 + eg: i * 8 + eg + 1], src,
                                    OP.mult, OP.add))

                        @block.vector
                        def _(v, i=i, bi=bi, half=half, gidx=gidx):
                            c.w(v, ba, 4 * (gidx + 1))
                            if gidx >= 1:
                                c.w(v, res, gidx)
                            if gidx >= 2:
                                AO.wait(v, gidx - 2)
                            v.drain()
                            if half == 1:
                                c.inc(res, v.tensor_tensor(
                                    acci[:, bi * D:(bi + 1) * D],
                                    acc[:, 1 * D:2 * D],
                                    xcb[:, bi * D:(bi + 1) * D], OP.add))
                            else:
                                c.inc(res, v.tensor_copy(
                                    acci[:, bi * D:(bi + 1) * D], acc[:, 1 * D:2 * D]))

                        @block.sync
                        def _(s, i=i, bi=bi, half=half, gidx=gidx):
                            c.w(s, res, gidx + 1)
                            dst = x_out if half == 1 else acc_hbm
                            AO.issue(gidx, s.dma_start(
                                out=dst[i * 128:(i + 1) * 128, :],
                                in_=acci[:, bi * D:(bi + 1) * D]))

                @block.sync
                def _(s):
                    AO.wait_all(s)
                    HL.wait_all(s)
                    AL.wait_all(s)

                barrier(c, nm(f"bB{half}"))


def emit_router(c, nt_, psr, rsc, w_se, rm, rva, rvs, rv):
    block = c.block

    @block.vector
    def _(v):
        c.w(v, rm, nt_ + 1)
        if nt_ >= 1:
            c.w(v, rv, nt_)
        v.tensor_reduce(rsc[:, 0:1], psr[:, 0:2], AX.X, OP.max)
        v.tensor_reduce(rsc[:, 5:6], psr[:, 8:12], AX.X, OP.max)
        v.tensor_reduce(rsc[:, 6:7], psr[:, 12:16], AX.X, OP.max)
        v.tensor_copy(rsc[:, 32:34], psr[:, 0:2])
        v.drain()
        v.tensor_tensor(rsc[:, 2:3], rsc[:, 33:34], rsc[:, 32:33], OP.is_gt)
        v.tensor_scalar(rsc[:, 1:2], rsc[:, 0:1], -1.0, None, OP.mult, OP.bypass)
        v.tensor_scalar(rsc[:, 3:4], rsc[:, 5:6], -1.0, None, OP.mult, OP.bypass)
        c.inc(rva, v.tensor_scalar(rsc[:, 4:5], rsc[:, 6:7], -1.0, None,
                                   OP.mult, OP.bypass))

    @block.scalar
    def _(s):
        c.w(s, rva, nt_ + 1)
        if nt_ >= 1:
            c.w(s, rvs, nt_)
        s.activation(rsc[:, 10:12], psr[:, 0:2], AF.Exp, bias=rsc[:, 1:2],
                     accum_out=rsc[:, 7:8])
        s.activation(rsc[:, 12:16], psr[:, 8:12], AF.Exp, bias=rsc[:, 3:4],
                     accum_out=rsc[:, 8:9])
        c.inc(rvs, s.activation(rsc[:, 16:20], psr[:, 12:16], AF.Exp,
                                bias=rsc[:, 4:5], accum_out=rsc[:, 9:10]))

    @block.vector
    def _(v):
        c.w(v, rvs, nt_ + 1)
        ws = w_se[:, nt_ * 8:(nt_ + 1) * 8]
        v.tensor_reduce(rsc[:, 20:21], rsc[:, 10:12], AX.X, OP.max)
        v.reciprocal(rsc[:, 21:22], rsc[:, 7:8])
        v.reciprocal(rsc[:, 22:23], rsc[:, 8:9])
        v.reciprocal(rsc[:, 23:24], rsc[:, 9:10])
        v.drain()
        v.tensor_scalar(rsc[:, 24:25], rsc[:, 20:21], rsc[:, 21:22], 0.5,
                        OP.mult, OP.mult)
        v.tensor_reduce(rsc[:, 27:28], rsc[:, 12:16], AX.X, OP.max)
        v.tensor_reduce(rsc[:, 28:29], rsc[:, 16:20], AX.X, OP.max)
        v.drain()
        v.tensor_scalar(rsc[:, 29:30], rsc[:, 2:3], -1.0, 1.0, OP.mult, OP.add)
        v.drain()
        v.scalar_tensor_tensor(rsc[:, 25:26], rsc[:, 29:30], rsc[:, 24:25],
                               rsc[:, 22:23], OP.mult, OP.mult)
        v.scalar_tensor_tensor(rsc[:, 26:27], rsc[:, 2:3], rsc[:, 24:25],
                               rsc[:, 23:24], OP.mult, OP.mult)
        v.tensor_scalar(ws[:, 0:4], rsc[:, 12:16], rsc[:, 27:28], None,
                        OP.is_equal, OP.bypass)
        v.tensor_scalar(ws[:, 4:8], rsc[:, 16:20], rsc[:, 28:29], None,
                        OP.is_equal, OP.bypass)
        v.drain()
        v.scalar_tensor_tensor(ws[:, 0:4], ws[:, 0:4], -1e30, rsc[:, 12:16],
                               OP.mult, OP.add)
        v.scalar_tensor_tensor(ws[:, 4:8], ws[:, 4:8], -1e30, rsc[:, 16:20],
                               OP.mult, OP.add)
        v.drain()
        v.tensor_reduce(rsc[:, 30:31], ws[:, 0:4], AX.X, OP.max)
        v.tensor_reduce(rsc[:, 31:32], ws[:, 4:8], AX.X, OP.max)
        v.drain()
        v.tensor_scalar(ws[:, 0:4], rsc[:, 12:16], rsc[:, 30:31], None,
                        OP.is_ge, OP.bypass)
        v.tensor_scalar(ws[:, 4:8], rsc[:, 16:20], rsc[:, 31:32], None,
                        OP.is_ge, OP.bypass)
        v.drain()
        v.scalar_tensor_tensor(ws[:, 0:4], rsc[:, 12:16], rsc[:, 25:26],
                               ws[:, 0:4], OP.mult, OP.mult)
        c.inc(rv, v.scalar_tensor_tensor(ws[:, 4:8], rsc[:, 16:20],
                                         rsc[:, 26:27], ws[:, 4:8],
                                         OP.mult, OP.mult))


def emit_l1a(c, tag, x_in, x_out, wk16, wv16, wr16, wo16, tmix, tdecay,
             tfirst, lng, lnb, ident_sb, eps_col, identf_sb):
    c.rebase()
    nc, block = c.nc, c.block
    nm = lambda n: f"{tag}{n}"
    vst, sst, xnr, trp, evc = map(nm, ["vst", "sst", "xnr", "trp", "evc"])
    wtl, pv, mxk, prj, pev, kvr, scn, orr, sqd, stm, str_, bcm, apl, wom, rsd = map(
        nm, ["wtl", "pv", "mxk", "prj", "pev", "kvr", "scn", "orr", "sqd",
             "stm", "str", "bcm", "apl", "wom", "rsd"])
    LD = DChan(c, nm("ld"), 2, 16)
    XC = DChan(c, nm("xc"), 2, 16)
    XO = DChan(c, nm("xo"), 2, 16)

    with ExitStack() as ps_:
        sb = lambda n, s, d: ps_.enter_context(nc.sbuf_tensor(f"{tag}{n}", s, d, allow_name_mangling=True))
        wkT = sb("wkT", [128, 8 * 1024], BF16)
        wvT = sb("wvT", [128, 8 * 1024], BF16)
        wrT = sb("wrT", [128, 8 * 1024], BF16)
        woT = sb("woT", [128, 8 * 1024], BF16)
        pcv = sb("pcv", [128, 80], F32)
        ones128f = sb("on1", [128, 1], F32)
        onesr = sb("onr", [1, 128], F32)
        zc = sb("zc", [128, 1], BF16)
        scar = sb("scar", [128, 8], F32)
        prep_stack = ExitStack()
        pld = prep_stack.enter_context(nc.sbuf_tensor(f"{tag}pld", [8, 6 * 128], F32, allow_name_mangling=True))
        ones512 = prep_stack.enter_context(nc.sbuf_tensor(f"{tag}on5", [128, 512], F32, allow_name_mangling=True))
        psV = prep_stack.enter_context(nc.psum_tensor(f"{tag}psV", [128, 80], F32, allow_name_mangling=True))

        @block.sync
        def _(s):
            s.wait_ge(c.sem("wkd"), c.val("wkd"))
            for w16, wT in ((wk16, wkT), (wv16, wvT), (wr16, wrT), (wo16, woT)):
                for cg in range(8):
                    c.inc(wtl, s.dma_start(
                        out=wT[:, cg * 1024:(cg + 1) * 1024],
                        in_=w16[:, cg * 128:(cg + 1) * 128], transpose=True), 16)
            srcs = [tmix[0], tmix[1], tmix[2], tdecay, tfirst, lng]
            for j, sv in enumerate(srcs):
                c.inc(wtl, s.dma_start(
                    out=pld[:, j * 128:(j + 1) * 128],
                    in_=sv.rearrange("(a b) -> a b", a=8)), 16)

        @block.tensor
        def _(t):
            t.wait_ge(c.sem(wtl), c.val(wtl))
            for j in range(6):
                inst = t.transpose(psV[:, j * 8:(j + 1) * 8],
                                   pld[:, j * 128:(j + 1) * 128],
                                   identf_sb[0:8, 0:8])
            c.inc(pv, inst)

        @block.vector
        def _(v):
            c.w(v, pv, 1)
            v.tensor_copy(pcv[:, 0:24], psV[:, 0:24])
            v.tensor_copy(pcv[:, 56:64], psV[:, 32:40])
            v.tensor_copy(pcv[:, 64:72], psV[:, 40:48])
            v.memset(ones512[:], 1.0)
            v.memset(ones128f[:], 1.0)
            v.memset(onesr[:], 1.0)
            v.memset(zc[:], 0.0)
            v.memset(scar[:], 0.0)
            v.drain()
            c.inc(pv, v.tensor_scalar(pcv[:, 24:48], pcv[:, 0:24], -1.0, 1.0,
                                      OP.mult, OP.add))

        @block.scalar
        def _(s):
            c.w(s, pv, 2)
            c.inc(pv, s.activation(pcv[:, 48:56], psV[:, 24:32], AF.Exp))

        @block.vector
        def _(v):
            c.w(v, pv, 3)
            c.inc(pv, v.engine_nop() if False else v.memset(zc[:], 0.0))

        @block.sync
        def _(s):
            c.w(s, pv, 1)
            c.inc(nm("wt2"), s.dma_start(out=pld[:, 0:128],
                                   in_=lnb.rearrange("(a b) -> a b", a=8)), 16)

        @block.tensor
        def _(t):
            t.wait_ge(c.sem(nm("wt2")), c.val(nm("wt2")))
            c.w(t, pv, 4)
            c.inc(pv, t.transpose(psV[:, 0:8], pld[:, 0:128],
                                  identf_sb[0:8, 0:8]))

        @block.vector
        def _(v):
            c.w(v, pv, 5)
            c.inc(pv, v.tensor_copy(pcv[:, 72:80], psV[:, 0:8]))

        barrier(c, nm("bP"))
        prep_stack.close()

        xs = sb("xs", [128, 2 * D], F32)
        xn16 = sb("xn16", [128, 2 * D], BF16)
        stat = sb("stat", [128, 16], F32)
        sq = sb("sq", [128, D], F32)
        xnTc = sb("xnTc", [128, 4096], BF16)
        ccol = sb("ccol", [128, 8], BF16)
        xmix = sb("xmix", [128, 8 * 512], BF16)
        P1 = sb("P1", [128, 4096], F32)
        P2 = sb("P2", [128, 4096], F32)
        P3 = sb("P3", [128, 4096], F32)
        ktmp = sb("ktmp", [128, 512], F32)
        rb = sb("rb", [128, 8 * 512], BF16)
        o16 = sb("o16", [128, 8 * 512], BF16)
        srow = sb("srow", [1, 4 * 512], F32)
        xcb = sb("xcb", [128, 2 * D], F32)
        accF = sb("accF", [128, 2 * D], F32)
        psT = ps_.enter_context(nc.psum_tensor(f"{tag}psT", [128, 1024], BF16, allow_name_mangling=True))
        psP = ps_.enter_context(nc.psum_tensor(f"{tag}psP", [128, 512], F32, allow_name_mangling=True))
        psS = ps_.enter_context(nc.psum_tensor(f"{tag}psS", [1, 1024], F32, allow_name_mangling=True))
        psB = ps_.enter_context(nc.psum_tensor(f"{tag}psB", [128, 1024], F32, allow_name_mangling=True))
        psO = ps_.enter_context(nc.psum_tensor(f"{tag}psO", [128, 1024], F32, allow_name_mangling=True))

        def mix(v, dst, scol, icol, xnT, prevcol):
            for cg in range(8):
                v.tensor_scalar(P1[:, cg * 512 + 1: cg * 512 + 512],
                                xnT[:, cg * 512: cg * 512 + 511],
                                pcv[:, icol + cg:icol + cg + 1], None,
                                OP.mult, OP.bypass)
                v.tensor_scalar(P1[:, cg * 512: cg * 512 + 1],
                                prevcol(cg),
                                pcv[:, icol + cg:icol + cg + 1], None,
                                OP.mult, OP.bypass)
            v.drain()
            for cg in range(8):
                inst = v.scalar_tensor_tensor(
                    dst[:, cg * 512:(cg + 1) * 512],
                    xnT[:, cg * 512:(cg + 1) * 512],
                    pcv[:, scol + cg:scol + cg + 1],
                    P1[:, cg * 512:(cg + 1) * 512], OP.mult, OP.add)
            return inst

        for ci in range(8):
            cb = ci % 2
            xnT = xnTc[:, 0:4096]

            @block.scalar
            def _(s, ci=ci):
                if ci >= 1:
                    c.w(s, mxk, 4 * ci)

            emit_ln_chunk(c, {"LD": LD, "vst": vst, "sst": sst, "xnr": xnr,
                              "trp": trp, "evc": evc}, ci, x_in, xnT,
                          (xs, xn16, stat, sq, psT), eps_col, ident_sb)

            def prevcol(cg, ci=ci):
                if ci == 0:
                    return zc[:]
                return ccol[:, cg:cg + 1]

            @block.vector
            def _(v, ci=ci, xnT=xnT):
                c.w(v, evc, 32 * (ci + 1))
                if ci >= 1:
                    c.w(v, prj, 24 * ci)
                    c.w(v, mxk, 4 * ci)
                    c.w(v, stm, 2 * ci)
                    c.w(v, sqd, 2 * ci - 1)
                c.inc(mxk, mix(v, xmix, 8, 32, xnT, prevcol))

            for dg in range(8):
                @block.tensor
                def _(t, ci=ci, dg=dg):
                    c.w(t, mxk, 4 * ci + 1)
                    n = ci * 24 + dg
                    if n >= 1:
                        c.w(t, pev, n)
                    for cg in range(8):
                        inst = t.matmul(psP[:],
                                        wvT[:, cg * 1024 + dg * 128:
                                            cg * 1024 + (dg + 1) * 128],
                                        xmix[:, cg * 512:(cg + 1) * 512],
                                        start=(cg == 0), stop=(cg == 7))
                    c.inc(prj, inst)

                @block.scalar
                def _(s, ci=ci, dg=dg):
                    n = ci * 24 + dg
                    c.w(s, prj, n + 1)
                    if dg == 0 and ci >= 1:
                        c.w(s, orr, 2 * ci)
                    c.inc(pev, s.copy(P2[:, dg * 512:(dg + 1) * 512], psP[:]))

            @block.vector
            def _(v, ci=ci, xnT=xnT):
                c.w(v, prj, 24 * ci + 8)
                c.inc(mxk, mix(v, xmix, 16, 40, xnT, prevcol))

            for dg in range(8):
                @block.tensor
                def _(t, ci=ci, dg=dg):
                    c.w(t, mxk, 4 * ci + 2)
                    n = ci * 24 + 8 + dg
                    c.w(t, pev, n)
                    for cg in range(8):
                        inst = t.matmul(psP[:],
                                        wrT[:, cg * 1024 + dg * 128:
                                            cg * 1024 + (dg + 1) * 128],
                                        xmix[:, cg * 512:(cg + 1) * 512],
                                        start=(cg == 0), stop=(cg == 7))
                    c.inc(prj, inst)

                @block.scalar
                def _(s, ci=ci, dg=dg):
                    n = ci * 24 + 8 + dg
                    c.w(s, prj, n + 1)
                    if dg == 0 and ci >= 1:
                        c.w(s, orr, 2 * ci)
                    c.inc(pev, s.copy(rb[:, dg * 512:(dg + 1) * 512], psP[:]))

            @block.vector
            def _(v, ci=ci, xnT=xnT):
                c.w(v, prj, 24 * ci + 16)
                c.inc(mxk, mix(v, xmix, 0, 24, xnT, prevcol))
                v.drain()
                for cg in range(8):
                    inst = v.tensor_copy(ccol[:, cg:cg + 1],
                                         xnT[:, cg * 512 + 511: cg * 512 + 512])
                c.inc(mxk, inst)

            for dg in range(8):
                @block.tensor
                def _(t, ci=ci, dg=dg):
                    c.w(t, mxk, 4 * ci + 3)
                    n = ci * 24 + 16 + dg
                    c.w(t, pev, n)
                    for cg in range(8):
                        inst = t.matmul(psP[:],
                                        wkT[:, cg * 1024 + dg * 128:
                                            cg * 1024 + (dg + 1) * 128],
                                        xmix[:, cg * 512:(cg + 1) * 512],
                                        start=(cg == 0), stop=(cg == 7))
                    c.inc(prj, inst)

                @block.scalar
                def _(s, ci=ci, dg=dg):
                    n = ci * 24 + 16 + dg
                    c.w(s, prj, n + 1)
                    c.w(s, pev, n)
                    if dg >= 1 or ci >= 1:
                        c.w(s, kvr, 8 * ci + dg)
                    c.inc(pev, s.copy(ktmp[:], psP[:]))

                @block.vector
                def _(v, ci=ci, dg=dg):
                    n = ci * 24 + 16 + dg
                    c.w(v, pev, n + 1)
                    if dg == 0:
                        c.w(v, mxk, 4 * ci + 3)
                    c.inc(kvr, v.tensor_tensor(P1[:, dg * 512:(dg + 1) * 512],
                                               ktmp[:],
                                               P2[:, dg * 512:(dg + 1) * 512],
                                               OP.mult))

            @block.vector
            def _(v, ci=ci):
                c.w(v, kvr, 8 * (ci + 1))
                v.drain()
                for dg in range(8):
                    v.tensor_tensor_scan(P3[:, dg * 512:(dg + 1) * 512],
                                         pcv[:, 48 + dg:49 + dg].broadcast_to(
                                             (128, 512)),
                                         P1[:, dg * 512:(dg + 1) * 512],
                                         scar[:, dg:dg + 1],
                                         OP.mult, OP.add)
                v.drain()
                for dg in range(8):
                    inst = v.tensor_copy(scar[:, dg:dg + 1],
                                         P3[:, dg * 512 + 511:(dg + 1) * 512])
                c.inc(scn, inst)

            @block.vector
            def _(v, ci=ci):
                c.w(v, scn, ci + 1)
                c.w(v, pev, 24 * (ci + 1))
                for dg in range(8):
                    inst = v.scalar_tensor_tensor(
                        P2[:, dg * 512:(dg + 1) * 512],
                        P1[:, dg * 512:(dg + 1) * 512],
                        pcv[:, 56 + dg:57 + dg],
                        P3[:, dg * 512:(dg + 1) * 512], OP.mult, OP.add)
                c.inc(orr, inst)
                v.drain()
                c.inc(orr, v.tensor_tensor(P1[:], P2[:], rb[:], OP.mult))

            @block.scalar
            def _(s, ci=ci):
                c.w(s, orr, 2 * (ci + 1))
                if ci >= 1:
                    c.w(s, stm, 2 * ci)
                c.inc(sqd, s.activation(P3[:], P1[:], AF.Square))

            @block.tensor
            def _(t, ci=ci):
                c.w(t, orr, 2 * (ci + 1))
                if ci >= 1:
                    c.w(t, str_, 2 * ci)
                for dg in range(8):
                    inst = t.matmul(psS[:, 0:512], ones128f[:],
                                    P1[:, dg * 512:(dg + 1) * 512],
                                    start=(dg == 0), stop=(dg == 7))
                c.inc(stm, inst)
                c.w(t, sqd, 2 * ci + 1)
                for dg in range(8):
                    inst = t.matmul(psS[:, 512:1024], ones128f[:],
                                    P3[:, dg * 512:(dg + 1) * 512],
                                    start=(dg == 0), stop=(dg == 7))
                c.inc(stm, inst)

            @block.vector
            def _(v, ci=ci):
                c.w(v, stm, 2 * (ci + 1))
                if ci >= 1:
                    c.w(v, bcm, ci)
                v.drain()
                v.tensor_scalar(srow[0:1, 512:1024], psS[0:1, 0:512],
                                1.0 / D, None, OP.mult, OP.bypass)
                v.drain()
                v.tensor_tensor(srow[0:1, 1024:1536], srow[0:1, 512:1024],
                                srow[0:1, 512:1024], OP.mult)
                v.drain()
                c.inc(str_, v.scalar_tensor_tensor(
                    srow[0:1, 0:512], psS[0:1, 512:1024], 1.0 / D,
                    srow[0:1, 1024:1536], OP.mult, OP.subtract))

            @block.scalar
            def _(s, ci=ci):
                c.w(s, str_, 2 * ci + 1)
                c.inc(sqd, s.activation(srow[0:1, 1536:2048], srow[0:1, 0:512],
                                        AF.Sqrt, bias=eps_col[0:1, :],
                                        scale=1.0))

            @block.vector
            def _(v, ci=ci):
                c.w(v, sqd, 2 * (ci + 1))
                v.reciprocal(srow[0:1, 0:512], srow[0:1, 1536:2048])
                v.drain()
                v.tensor_scalar(srow[0:1, 1024:1536], srow[0:1, 512:1024],
                                -1.0, None, OP.mult, OP.bypass)
                v.drain()
                c.inc(str_, v.tensor_tensor(srow[0:1, 1024:1536],
                                            srow[0:1, 1024:1536],
                                            srow[0:1, 0:512], OP.mult))

            @block.tensor
            def _(t, ci=ci):
                c.w(t, str_, 2 * (ci + 1))
                if ci >= 1:
                    c.w(t, apl, ci)
                t.matmul(psB[:, 0:512], onesr[:], srow[0:1, 0:512],
                         start=True, stop=True)
                inst = t.matmul(psB[:, 512:1024], onesr[:],
                                srow[0:1, 1024:1536], start=True, stop=True)
                c.inc(bcm, inst)

            @block.vector
            def _(v, ci=ci):
                c.w(v, bcm, ci + 1)
                if ci >= 1:
                    c.w(v, wom, 4 * ci)
                for dg in range(8):
                    v.tensor_tensor(P2[:, dg * 512:(dg + 1) * 512],
                                    P1[:, dg * 512:(dg + 1) * 512],
                                    psB[:, 0:512], OP.mult)
                v.drain()
                for dg in range(8):
                    v.tensor_tensor(P3[:, dg * 512:(dg + 1) * 512],
                                    P2[:, dg * 512:(dg + 1) * 512],
                                    psB[:, 512:1024], OP.add)
                v.drain()
                for dg in range(8):
                    inst = v.tensor_scalar(o16[:, dg * 512:(dg + 1) * 512],
                                           P3[:, dg * 512:(dg + 1) * 512],
                                           pcv[:, 64 + dg:65 + dg],
                                           pcv[:, 72 + dg:73 + dg],
                                           OP.mult, OP.add)
                c.inc(apl, inst)

            for ts in range(4):
                n = ci * 4 + ts

                @block.sync
                def _(s, n=n):
                    if n >= 2:
                        c.w(s, rsd, n - 1)
                    XC.issue(n, s.dma_start(
                        out=xcb[:, (n % 2) * D:(n % 2 + 1) * D],
                        in_=x_in[n * 128:(n + 1) * 128, :]))

                @block.tensor
                def _(t, ci=ci, ts=ts, n=n):
                    c.w(t, apl, ci + 1)
                    if n >= 1:
                        c.w(t, rsd, n)
                    for dh in range(2):
                        for cg in range(8):
                            inst = t.matmul(
                                psO[:, dh * 512:(dh + 1) * 512],
                                o16[:, cg * 512 + ts * 128: cg * 512 + (ts + 1) * 128],
                                woT[:, cg * 1024 + dh * 512: cg * 1024 + (dh + 1) * 512],
                                start=(cg == 0), stop=(cg == 7))
                    c.inc(wom, inst)

                @block.vector
                def _(v, ci=ci, ts=ts, n=n):
                    c.w(v, wom, n + 1)
                    XC.wait(v, n)
                    if n >= 1:
                        c.w(v, rsd, n)
                    if n >= 2:
                        XO.wait(v, n - 2)
                    c.inc(rsd, v.tensor_tensor(
                        accF[:, (n % 2) * D:(n % 2 + 1) * D], psO[:],
                        xcb[:, (n % 2) * D:(n % 2 + 1) * D], OP.add))

                @block.sync
                def _(s, n=n):
                    c.w(s, rsd, n + 1)
                    XO.issue(n, s.dma_start(
                        out=x_out[n * 128:(n + 1) * 128, :],
                        in_=accF[:, (n % 2) * D:(n % 2 + 1) * D]))

        @block.sync
        def _(s):
            XO.wait_all(s)
            XC.wait_all(s)
            LD.wait_all(s)

        barrier(c, nm("bL"))


def emit_l0a(c, tag, x_in, x_out, wq16a, wk16a, wv16a, masks, ident_sb,
             eps_col):
    c.rebase()
    nc, block = c.nc, c.block
    nm = lambda n: f"{tag}{n}"
    vst, sst, xnr, trp, evc = map(nm, ["vst", "sst", "xnr", "trp", "evc"])
    wtl, ini, qkm, qke, vvm, vve, dtm, sfx, exe, psc, trs, tre, oum, rsd, cry = map(
        nm, ["wtl", "ini", "qkm", "qke", "vvm", "vve", "dtm", "sfx", "exe",
             "psc", "trs", "tre", "oum", "rsd", "cry"])
    LD = DChan(c, nm("ld"), 2, 16)
    XC = DChan(c, nm("xc"), 2, 16)
    XO = DChan(c, nm("xo"), 2, 16)

    NSC = int(os.environ.get("NSC", "8"))
    AP2 = os.environ.get("AP2", "qvdmsteo")
    with ExitStack() as ps_:
        sb = lambda n, s, d: ps_.enter_context(nc.sbuf_tensor(f"{tag}{n}", s, d, allow_name_mangling=True))
        wq_sb = sb("wq", [128, 8 * 1024], BF16)
        wk_sb = sb("wk", [128, 8 * 1024], BF16)
        wv_sb = sb("wv", [128, 8 * 1024], BF16)
        mk_sb = sb("mk", [64, 256], BF16)
        xs = sb("xs", [128, 2 * D], F32)
        xn16 = sb("xn16", [128, 2 * D], BF16)
        stat = sb("stat", [128, 16], F32)
        sq = sb("sq", [128, D], F32)
        xnTc = sb("xnTc", [128, 4096], BF16)
        qTh = sb("qTh", [64, 16 * 512], BF16)
        kTh = sb("kTh", [64, 16 * 576], BF16)
        vch = sb("vch", [64, 10 * 1024], BF16)
        dm = sb("dm", [64, 512], F32)
        dmx = sb("dmx", [64, 16], F32)
        prb = sb("prb", [64, 512], BF16)
        prb2 = sb("prb2", [64, 512], BF16)
        La = sb("La", [64, 2 * 2048], BF16)
        Lm = sb("Lm", [64, 2 * 2048], BF16)
        Lb = sb("Lb", [64, 2 * 2048], BF16)
        xcb = sb("xcb", [128, 2 * D], F32)
        accF = sb("accF", [128, 2 * D], F32)
        psT = ps_.enter_context(nc.psum_tensor(f"{tag}psT", [128, 1024], BF16, allow_name_mangling=True))
        psQK = ps_.enter_context(nc.psum_tensor(f"{tag}psQK", [128, 2 * 512], F32, allow_name_mangling=True))
        psV2 = ps_.enter_context(nc.psum_tensor(f"{tag}psV2", [128, 512], F32, allow_name_mangling=True))
        psD = ps_.enter_context(nc.psum_tensor(f"{tag}psD", [64, 2 * 512], F32, allow_name_mangling=True))
        psO = ps_.enter_context(nc.psum_tensor(f"{tag}psO", [128, 1024], F32, allow_name_mangling=True))

        @block.sync
        def _(s):
            s.wait_ge(c.sem("wkd"), c.val("wkd"))
            for w16, wsb in ((wq16a, wq_sb), (wk16a, wk_sb), (wv16a, wv_sb)):
                for cg in range(8):
                    c.inc(wtl, s.dma_start(
                        out=wsb[:, cg * 1024:(cg + 1) * 1024],
                        in_=w16[cg * 128:(cg + 1) * 128, :]), 16)
            c.inc(wtl, s.dma_start(out=mk_sb[:], in_=masks[:, :]), 16)

        @block.vector
        def _(v):
            v.memset(kTh[:], 0.0)
            v.memset(vch[:], 0.0)
            v.memset(La[:], 0.0)
            c.inc(ini, v.memset(Lb[:], 0.0))

        @block.tensor
        def _(t):
            t.wait_ge(c.sem(wtl), c.val(wtl))
            c.w(t, ini, 1)

        @block.scalar
        def _(s):
            c.w(s, ini, 1)

        for sc in range(NSC):
            xnT = xnTc[:, 0:4096]

            @block.scalar
            def _(s, sc=sc):
                if sc >= 1:
                    c.w(s, qkm, 16 * sc)
                    c.w(s, vvm, 8 * sc)

            emit_ln_chunk(c, {"LD": LD, "vst": vst, "sst": sst, "xnr": xnr,
                              "trp": trp, "evc": evc}, sc, x_in, xnT,
                          (xs, xn16, stat, sq, psT), eps_col, ident_sb,
                          tensor_waits0=[lambda t, sc=sc: c.w(t, tre, 32 * sc)]
                          if sc >= 1 else ())

            @block.vector
            def _(v, sc=sc):
                if sc >= 1:
                    c.w(v, oum, 4 * sc)
                    c.w(v, cry, sc)
                    c.w(v, dtm, 32 * sc)
                    kv = kTh[:].rearrange("p (h w) -> p h w", w=576)
                    v.tensor_copy(kv[:, :, 0:64], kv[:, :, 512:576])
                    c.inc(cry, v.tensor_copy(vch[:, 0:1024],
                                             vch[:, 8 * 1024:9 * 1024]))
                else:
                    c.inc(cry, v.memset(dmx[:], 0.0))

            for mi, wsb in enumerate((wq_sb, wk_sb) if "q" in AP2 else ()):
                for dg in range(8):
                    @block.tensor
                    def _(t, sc=sc, mi=mi, dg=dg, wsb=wsb):
                        c.w(t, evc, 32 * (sc + 1))
                        if mi == 1 and dg == 0:
                            c.w(t, cry, sc + 1)
                        n = sc * 16 + mi * 8 + dg
                        if n >= 2:
                            c.w(t, qke, 2 * (n - 1))
                        pq = psQK[:, (n % 2) * 512:(n % 2 + 1) * 512]
                        for cg in range(8):
                            inst = t.matmul(pq,
                                            wsb[:, cg * 1024 + dg * 128:
                                                cg * 1024 + (dg + 1) * 128],
                                            xnT[:, cg * 512:(cg + 1) * 512],
                                            start=(cg == 0), stop=(cg == 7))
                        c.inc(qkm, inst)

                    @block.vector
                    def _(v, sc=sc, mi=mi, dg=dg):
                        n = sc * 16 + mi * 8 + dg
                        c.w(v, qkm, n + 1)
                        pq = psQK[:, (n % 2) * 512:(n % 2 + 1) * 512]
                        if mi == 0:
                            if sc >= 1 and dg == 0:
                                c.w(v, dtm, 32 * sc)
                            for hh in range(2):
                                h = 2 * dg + hh
                                c.inc(qke, v.tensor_copy(
                                    qTh[:, h * 512:(h + 1) * 512],
                                    pq[hh * 64:(hh + 1) * 64, :]))
                        else:
                            for hh in range(2):
                                h = 2 * dg + hh
                                c.inc(qke, v.tensor_scalar(
                                    kTh[:, h * 576 + 64:(h + 1) * 576],
                                    pq[hh * 64:(hh + 1) * 64, :],
                                    0.125, None, OP.mult, OP.bypass))

            for ts in range(4 if "v" in AP2 else 0):
                for dh in range(2):
                    @block.tensor
                    def _(t, sc=sc, ts=ts, dh=dh):
                        c.w(t, evc, 32 * (sc + 1))
                        if ts == 0 and dh == 0:
                            c.w(t, cry, sc + 1)
                        n = (sc * 4 + ts) * 2 + dh
                        if n >= 1:
                            c.w(t, vve, 2 * n)
                        for cg in range(8):
                            inst = t.matmul(
                                psV2[:],
                                xnT[:, cg * 512 + ts * 128: cg * 512 + (ts + 1) * 128],
                                wv_sb[:, cg * 1024 + dh * 512: cg * 1024 + (dh + 1) * 512],
                                start=(cg == 0), stop=(cg == 7))
                        c.inc(vvm, inst)

                    @block.vector
                    def _(v, sc=sc, ts=ts, dh=dh):
                        n = (sc * 4 + ts) * 2 + dh
                        c.w(v, vvm, n + 1)
                        if sc >= 1 and ts == 0 and dh == 0:
                            c.w(v, oum, 4 * sc)
                        for hh in range(2):
                            j = 2 * ts + hh
                            c.inc(vve, v.tensor_copy(
                                vch[:, (1 + j) * 1024 + dh * 512:
                                    (1 + j) * 1024 + (dh + 1) * 512],
                                psV2[hh * 64:(hh + 1) * 64, :]))

            for j in range(8 if "d" in AP2 else 0):
                for hg in range(4):
                    nb = (sc * 8 + j) * 4 + hg

                    @block.tensor
                    def _(t, sc=sc, j=j, hg=hg, nb=nb):
                        c.w(t, qke, 32 * (sc + 1))
                        if nb >= 2 and "m" in AP2:
                            c.w(t, sfx, 2 * (nb - 1))
                        mks = mk_sb[:, 0:128] if not (sc == 0 and j == 0) \
                            else mk_sb[:, 128:256]
                        pd = psD[:, (nb % 2) * 512:(nb % 2 + 1) * 512]
                        for hh in range(4):
                            h = hg * 4 + hh
                            t.matmul(pd[:, hh * 128:(hh + 1) * 128],
                                     qTh[:, h * 512 + j * 64: h * 512 + (j + 1) * 64],
                                     kTh[:, h * 576 + j * 64: h * 576 + j * 64 + 128],
                                     start=True, stop=False)
                            inst = t.matmul(pd[:, hh * 128:(hh + 1) * 128],
                                            ident_sb[0:64, 0:64], mks,
                                            start=False, stop=True)
                        c.inc(dtm, inst)

                    if "m" not in AP2:
                        continue

                    @block.vector
                    def _(v, nb=nb):
                        c.w(v, dtm, nb + 1)
                        if nb >= 1:
                            c.w(v, sfx, 2 * nb)
                        d3 = psD[:, (nb % 2) * 512:(nb % 2 + 1) * 512].rearrange(
                            "p (a b) -> p a b", b=128)
                        v.tensor_reduce(dmx[:, 0:4], d3, AX.X, OP.max)
                        v.drain()
                        v.tensor_scalar(dmx[:, 4:8], dmx[:, 0:4], -1.0, None,
                                        OP.mult, OP.bypass)
                        v.drain()
                        m3 = dm[:].rearrange("p (a b) -> p a b", b=128)
                        c.inc(sfx, v.tensor_tensor(
                            m3, d3, dmx[:, 4:8].unsqueeze(2).broadcast_to(
                                (64, 4, 128)), OP.add))

                    if "s" not in AP2:
                        continue

                    @block.scalar
                    def _(s, nb=nb):
                        c.w(s, sfx, 2 * nb + 1)
                        if nb >= 1:
                            c.w(s, exe, nb)
                        c.inc(exe, s.activation(prb[:], dm[:], AF.Exp))

                    @block.vector
                    def _(v, nb=nb):
                        c.w(v, exe, nb + 1)
                        if nb >= 1:
                            c.w(v, trs, nb)
                        p3 = prb[:].rearrange("p (a b) -> p a b", b=128)
                        v.tensor_reduce(dmx[:, 8:12], p3, AX.X, OP.add)
                        v.drain()
                        v.reciprocal(dmx[:, 12:16], dmx[:, 8:12])
                        v.drain()
                        for hh in range(4):
                            inst = v.tensor_scalar(
                                prb2[:, hh * 128:(hh + 1) * 128],
                                prb[:, hh * 128:(hh + 1) * 128],
                                dmx[:, 12 + hh:13 + hh], None, OP.mult,
                                OP.bypass)
                            if hh == 2:
                                c.inc(sfx, inst)
                        c.inc(psc, inst)

                    if "t" not in AP2:
                        continue

                    @block.tensor
                    def _(t, nb=nb):
                        c.w(t, psc, nb + 1)
                        if nb >= 2:
                            c.w(t, tre, nb - 1)
                        pt2 = psT[:, (nb % 2) * 256:(nb % 2 + 1) * 256]
                        for hh in range(4):
                            inst = t.transpose(pt2[:, hh * 64:(hh + 1) * 64],
                                               prb2[:, hh * 128:(hh + 1) * 128],
                                               ident_sb[0:64, 0:64])
                        c.inc(trs, inst)

                    if "e" not in AP2:
                        continue

                    @block.vector
                    def _(v, sc=sc, j=j, hg=hg, nb=nb):
                        c.w(v, trs, nb + 1)
                        b = j // 2
                        bp = b % 2
                        nblk = sc * 4 + b
                        if nblk >= 2:
                            c.w(v, oum, nblk - 1)
                        pt2 = psT[:, (nb % 2) * 256:(nb % 2 + 1) * 256]
                        pa = pt2[0:64, :].rearrange("p (a b) -> p a b", b=64)
                        pb = pt2[64:128, :].rearrange("p (a b) -> p a b", b=64)
                        if j % 2 == 0:
                            dsta, dstb = La, Lm
                            lo, hi = 0, 64
                        else:
                            dsta, dstb = Lm, Lb
                            lo, hi = 64, 128
                        da = dsta[:, bp * 2048 + hg * 4 * 128:
                                  bp * 2048 + (hg * 4 + 4) * 128].rearrange(
                            "p (a b) -> p a b", b=128)[:, :, lo:lo + 64]
                        db = dstb[:, bp * 2048 + hg * 4 * 128:
                                  bp * 2048 + (hg * 4 + 4) * 128].rearrange(
                            "p (a b) -> p a b", b=128)[:, :, lo:lo + 64]
                        v.tensor_copy(da, pa)
                        c.inc(tre, v.tensor_copy(db, pb))

                if j % 2 == 1 and "o" in AP2:
                    b = j // 2
                    nblk = sc * 4 + b
                    bp = b % 2

                    @block.sync
                    def _(s, nblk=nblk):
                        if nblk >= 2:
                            c.w(s, rsd, nblk - 1)
                        XC.issue(nblk, s.dma_start(
                            out=xcb[:, (nblk % 2) * D:(nblk % 2 + 1) * D],
                            in_=x_in[nblk * 128:(nblk + 1) * 128, :]))

                    @block.tensor
                    def _(t, sc=sc, j=j, b=b, bp=bp, nblk=nblk):
                        c.w(t, tre, (sc * 8 + j) * 4 + 4)
                        if nblk >= 1:
                            c.w(t, rsd, nblk)
                        for h in range(16):
                            for li, Lx in enumerate((La, Lm, Lb)):
                                m = 2 * b - 1 + li
                                vsl = vch[:, (1 + m) * 1024 + h * 64:
                                          (1 + m) * 1024 + (h + 1) * 64]
                                inst = t.matmul(
                                    psO[:, h * 64:(h + 1) * 64],
                                    Lx[:, bp * 2048 + h * 128:
                                       bp * 2048 + (h + 1) * 128],
                                    vsl,
                                    start=(li == 0), stop=(li == 2))
                        c.inc(oum, inst)

                    @block.vector
                    def _(v, nblk=nblk):
                        c.w(v, oum, nblk + 1)
                        XC.wait(v, nblk)
                        if nblk >= 1:
                            c.w(v, rsd, nblk)
                        if nblk >= 2:
                            XO.wait(v, nblk - 2)
                        c.inc(rsd, v.tensor_tensor(
                            accF[:, (nblk % 2) * D:(nblk % 2 + 1) * D], psO[:],
                            xcb[:, (nblk % 2) * D:(nblk % 2 + 1) * D], OP.add))

                    @block.sync
                    def _(s, nblk=nblk):
                        c.w(s, rsd, nblk + 1)
                        XO.issue(nblk, s.dma_start(
                            out=x_out[nblk * 128:(nblk + 1) * 128, :],
                            in_=accF[:, (nblk % 2) * D:(nblk % 2 + 1) * D]))

        @block.sync
        def _(s):
            XO.wait_all(s)
            XC.wait_all(s)
            LD.wait_all(s)

        barrier(c, nm("bA0"))


def build_l0f_program(stages="ABC", with_keys=True):
    nc = bass.Bass("TRN2", target_bir_lowering=False)
    x = nc.dram_tensor("x", [T, D], F32, kind="ExternalInput").ap()
    w_q = nc.dram_tensor("w_q", [D, 8192], F32, kind="ExternalInput").ap()
    keys = nc.dram_tensor("keys", [8, 256, 2, 512], F32, kind="ExternalInput").ap()
    w_down = nc.dram_tensor("w_down", [65536, D], F32, kind="ExternalInput").ap()
    w_up = nc.dram_tensor("w_up", [65536, D], F32, kind="ExternalInput").ap()
    iota = nc.dram_tensor("iota", [128, 256], F32, kind="ExternalInput").ap()
    ident = nc.dram_tensor("ident", [128, 128], BF16, kind="ExternalInput").ap()
    xout = nc.dram_tensor("xout", [T, D], F32, kind="ExternalOutput").ap()
    keys16 = nc.dram_tensor("keys16", [8, 256, 2, 512], BF16, kind="Internal").ap()
    w16cat = nc.dram_tensor("w16cat", [65536, 2048], BF16, kind="Internal").ap()
    wq16h = nc.dram_tensor("wq16h", [D, 8192], BF16, kind="Internal").ap()
    xn_hbm = nc.dram_tensor("xn_hbm", [T, D], BF16, kind="Internal").ap()

    with ExitStack() as stack:
        iota_sb = stack.enter_context(nc.sbuf_tensor("iotas", [128, 256], F32, allow_name_mangling=True))
        ident_sb = stack.enter_context(nc.sbuf_tensor("idents", [128, 128], BF16, allow_name_mangling=True))
        eps_col = stack.enter_context(nc.sbuf_tensor("epsc", [128, 1], F32, allow_name_mangling=True))
        castbuf = stack.enter_context(nc.sbuf_tensor("castbuf", [128, 8192], BF16, allow_name_mangling=True))
        block = stack.enter_context(nc.Block())
        c = Ctx(nc, block, stack)

        @block.sync
        def _(s):
            c.inc("cst", s.dma_start(out=iota_sb[:], in_=iota[:, :]), 16)
            c.inc("cst", s.dma_start(out=ident_sb[:], in_=ident[:, :]), 16)
            s.wait_ge(c.sem("cst"), c.val("cst"))

        @block.vector
        def _(v):
            v.memset(eps_col[:], LN_EPS)

        kg = emit_keys_cast(c, keys, keys16, castbuf)
        wqg = emit_cast16(c, "w", [(w_q, wq16h)], castbuf)[0]
        cg = emit_cat_cast(c, w_down, w_up, w16cat, castbuf)
        emit_l0f(c, "f0", x, xout, xn_hbm, wq16h, wqg, keys16, w16cat, cg, kg,
                 iota_sb, ident_sb, eps_col, stages=stages)
    return nc


def build_l1f_program():
    nc = bass.Bass("TRN2", target_bir_lowering=False)
    x = nc.dram_tensor("x", [T, D], F32, kind="ExternalInput").ap()
    w_in = nc.dram_tensor("w_in", [4096, D], F32, kind="ExternalInput").ap()
    w_out = nc.dram_tensor("w_out", [D, 2048], F32, kind="ExternalInput").ap()
    w1 = nc.dram_tensor("w1", [4096, D], F32, kind="ExternalInput").ap()
    w2 = nc.dram_tensor("w2", [8, D, 2048], F32, kind="ExternalInput").ap()
    wg = nc.dram_tensor("wg", [2, D], F32, kind="ExternalInput").ap()
    we = nc.dram_tensor("we", [8, D], F32, kind="ExternalInput").ap()
    gb = nc.dram_tensor("gb", [2], F32, kind="ExternalInput").ap()
    eb = nc.dram_tensor("eb", [8], F32, kind="ExternalInput").ap()
    ident = nc.dram_tensor("ident", [128, 128], BF16, kind="ExternalInput").ap()
    xout = nc.dram_tensor("xout", [T, D], F32, kind="ExternalOutput").ap()
    w_in16 = nc.dram_tensor("w_in16", [4096, D], BF16, kind="Internal").ap()
    w_out16 = nc.dram_tensor("w_out16", [D, 2048], BF16, kind="Internal").ap()
    w1_16 = nc.dram_tensor("w1_16", [4096, D], BF16, kind="Internal").ap()
    w2_16 = nc.dram_tensor("w2_16", [8, D, 2048], BF16, kind="Internal").ap()
    wg16 = nc.dram_tensor("wg16", [2, D], BF16, kind="Internal").ap()
    we16 = nc.dram_tensor("we16", [8, D], BF16, kind="Internal").ap()
    hT_hbm = nc.dram_tensor("hT_hbm", [16, 128, T], BF16, kind="Internal").ap()
    acc_hbm = nc.dram_tensor("acc_hbm", [T, D], F32, kind="Internal").ap()
    xnT_hbm = nc.dram_tensor("xnT_hbm", [8, 128, T], BF16, kind="Internal").ap()

    with ExitStack() as stack:
        ident_sb = stack.enter_context(nc.sbuf_tensor("idents", [128, 128], BF16, allow_name_mangling=True))
        eps_col = stack.enter_context(nc.sbuf_tensor("epsc", [128, 1], F32, allow_name_mangling=True))
        ones1_sb = stack.enter_context(nc.sbuf_tensor("ones1", [1, 128], BF16, allow_name_mangling=True))
        block = stack.enter_context(nc.Block())
        c = Ctx(nc, block, stack)

        @block.sync
        def _(s):
            c.inc("cst", s.dma_start(out=ident_sb[:], in_=ident[:, :]), 16)
            s.wait_ge(c.sem("cst"), c.val("cst"))

        @block.vector
        def _(v):
            v.memset(eps_col[:], LN_EPS)

        castbuf = stack.enter_context(nc.sbuf_tensor("castbuf", [128, 8192], BF16, allow_name_mangling=True))
        emit_cast16(c, "w", [(w_in, w_in16), (w_out, w_out16), (w1, w1_16),
                             (w2, w2_16), (wg, wg16), (we, we16)], castbuf)
        emit_l1f(c, "m0", x, xout, w_in16, w_out16, w1_16, w2_16, wg16, we16,
                 gb, eb, hT_hbm, acc_hbm, xnT_hbm, ident_sb, eps_col, ones1_sb)
    return nc


def build_l1a_program():
    nc = bass.Bass("TRN2", target_bir_lowering=False)
    x = nc.dram_tensor("x", [T, D], F32, kind="ExternalInput").ap()
    wk = nc.dram_tensor("wk", [D, D], F32, kind="ExternalInput").ap()
    wv = nc.dram_tensor("wv", [D, D], F32, kind="ExternalInput").ap()
    wr = nc.dram_tensor("wr", [D, D], F32, kind="ExternalInput").ap()
    wo = nc.dram_tensor("wo", [D, D], F32, kind="ExternalInput").ap()
    tmix = nc.dram_tensor("tmix", [3, D], F32, kind="ExternalInput").ap()
    tdecay = nc.dram_tensor("tdecay", [D], F32, kind="ExternalInput").ap()
    tfirst = nc.dram_tensor("tfirst", [D], F32, kind="ExternalInput").ap()
    lng = nc.dram_tensor("lng", [D], F32, kind="ExternalInput").ap()
    lnb = nc.dram_tensor("lnb", [D], F32, kind="ExternalInput").ap()
    ident = nc.dram_tensor("ident", [128, 128], BF16, kind="ExternalInput").ap()
    identf = nc.dram_tensor("identf", [128, 128], F32, kind="ExternalInput").ap()
    xout = nc.dram_tensor("xout", [T, D], F32, kind="ExternalOutput").ap()
    wk16 = nc.dram_tensor("wk16", [D, D], BF16, kind="Internal").ap()
    wv16 = nc.dram_tensor("wv16", [D, D], BF16, kind="Internal").ap()
    wr16 = nc.dram_tensor("wr16", [D, D], BF16, kind="Internal").ap()
    wo16 = nc.dram_tensor("wo16", [D, D], BF16, kind="Internal").ap()

    with ExitStack() as stack:
        ident_sb = stack.enter_context(nc.sbuf_tensor("idents", [128, 128], BF16, allow_name_mangling=True))
        identf_sb = stack.enter_context(nc.sbuf_tensor("identfs", [128, 128], F32, allow_name_mangling=True))
        eps_col = stack.enter_context(nc.sbuf_tensor("epsc", [128, 1], F32, allow_name_mangling=True))
        block = stack.enter_context(nc.Block())
        c = Ctx(nc, block, stack)

        @block.sync
        def _(s):
            c.inc("cst", s.dma_start(out=ident_sb[:], in_=ident[:, :]), 16)
            c.inc("cst", s.dma_start(out=identf_sb[:], in_=identf[:, :]), 16)
            s.wait_ge(c.sem("cst"), c.val("cst"))

        @block.vector
        def _(v):
            v.memset(eps_col[:], LN_EPS)

        castbuf = stack.enter_context(nc.sbuf_tensor("castbuf", [128, 8192], BF16, allow_name_mangling=True))
        emit_cast16(c, "w", [(wk, wk16), (wv, wv16), (wr, wr16), (wo, wo16)], castbuf)
        emit_l1a(c, "a1", x, xout, wk16, wv16, wr16, wo16, tmix, tdecay,
                 tfirst, lng, lnb, ident_sb, eps_col, identf_sb)
    return nc


def build_l0a_program():
    nc = bass.Bass("TRN2", target_bir_lowering=False)
    x = nc.dram_tensor("x", [T, D], F32, kind="ExternalInput").ap()
    wq = nc.dram_tensor("wq", [D, D], F32, kind="ExternalInput").ap()
    wk = nc.dram_tensor("wk", [D, D], F32, kind="ExternalInput").ap()
    wv = nc.dram_tensor("wv", [D, D], F32, kind="ExternalInput").ap()
    masks = nc.dram_tensor("masks", [64, 256], BF16, kind="ExternalInput").ap()
    ident = nc.dram_tensor("ident", [128, 128], BF16, kind="ExternalInput").ap()
    xout = nc.dram_tensor("xout", [T, D], F32, kind="ExternalOutput").ap()
    wq16 = nc.dram_tensor("wq16", [D, D], BF16, kind="Internal").ap()
    wk16 = nc.dram_tensor("wk16", [D, D], BF16, kind="Internal").ap()
    wv16 = nc.dram_tensor("wv16", [D, D], BF16, kind="Internal").ap()

    with ExitStack() as stack:
        ident_sb = stack.enter_context(nc.sbuf_tensor("idents", [128, 128], BF16, allow_name_mangling=True))
        eps_col = stack.enter_context(nc.sbuf_tensor("epsc", [128, 1], F32, allow_name_mangling=True))
        block = stack.enter_context(nc.Block())
        c = Ctx(nc, block, stack)

        @block.sync
        def _(s):
            c.inc("cst", s.dma_start(out=ident_sb[:], in_=ident[:, :]), 16)
            s.wait_ge(c.sem("cst"), c.val("cst"))

        @block.vector
        def _(v):
            v.memset(eps_col[:], LN_EPS)

        castbuf = stack.enter_context(nc.sbuf_tensor("castbuf", [128, 8192], BF16, allow_name_mangling=True))
        emit_cast16(c, "w", [(wq, wq16), (wk, wk16), (wv, wv16)], castbuf)
        emit_l0a(c, "a0", x, xout, wq16, wk16, wv16, masks, ident_sb, eps_col)
    return nc


def make_attn_masks():
    m = np.zeros((64, 256), np.float32)
    iu = np.triu_indices(64, 1)
    m[:, 64:128][iu] = -1e9
    m[:, 128:192] = -1e9
    m[:, 192:256][iu] = -1e9
    return m.astype(ml_dtypes.bfloat16)


def make_consts():
    iota = np.broadcast_to(np.arange(256, dtype=np.float32), (128, 256)).copy()
    ident = np.eye(128, dtype=ml_dtypes.bfloat16)
    return iota, ident


def build_fused_program():
    nc = bass.Bass("TRN2", target_bir_lowering=False)
    dr = lambda n, s, d, k: nc.dram_tensor(n, s, d, kind=k).ap()
    x = dr("x", [T, D], F32, "ExternalInput")
    xout = dr("xout", [T, D], F32, "ExternalOutput")
    w_q = dr("w_q", [D, 8192], F32, "ExternalInput")
    keys = dr("keys", [8, 256, 2, 512], F32, "ExternalInput")
    w_down = dr("w_down", [65536, D], F32, "ExternalInput")
    w_up = dr("w_up", [65536, D], F32, "ExternalInput")
    wq0 = dr("wq0", [D, D], F32, "ExternalInput")
    wk0 = dr("wk0", [D, D], F32, "ExternalInput")
    wv0 = dr("wv0", [D, D], F32, "ExternalInput")
    masks = dr("masks", [64, 256], BF16, "ExternalInput")
    w_in = dr("w_in", [4096, D], F32, "ExternalInput")
    w_out = dr("w_out", [D, 2048], F32, "ExternalInput")
    w1 = dr("w1", [4096, D], F32, "ExternalInput")
    w2 = dr("w2", [8, D, 2048], F32, "ExternalInput")
    wg = dr("wg", [2, D], F32, "ExternalInput")
    we = dr("we", [8, D], F32, "ExternalInput")
    gb = dr("gb", [2], F32, "ExternalInput")
    eb = dr("eb", [8], F32, "ExternalInput")
    wk1 = dr("wk1", [D, D], F32, "ExternalInput")
    wv1 = dr("wv1", [D, D], F32, "ExternalInput")
    wr1 = dr("wr1", [D, D], F32, "ExternalInput")
    wo1 = dr("wo1", [D, D], F32, "ExternalInput")
    tmix = dr("tmix", [3, D], F32, "ExternalInput")
    tdecay = dr("tdecay", [D], F32, "ExternalInput")
    tfirst = dr("tfirst", [D], F32, "ExternalInput")
    lng = dr("lng", [D], F32, "ExternalInput")
    lnb = dr("lnb", [D], F32, "ExternalInput")
    iota = dr("iota", [128, 256], F32, "ExternalInput")
    ident = dr("ident", [128, 128], BF16, "ExternalInput")
    identf = dr("identf", [128, 128], F32, "ExternalInput")
    xa = dr("xa", [T, D], F32, "Internal")
    xb = dr("xb", [T, D], F32, "Internal")
    xn_hbm = dr("xn_hbm", [T, D], BF16, "Internal")
    keys16 = dr("keys16", [8, 256, 2, 512], BF16, "Internal")
    wq16a = dr("wq16a", [D, D], BF16, "Internal")
    wk16a = dr("wk16a", [D, D], BF16, "Internal")
    wv16a = dr("wv16a", [D, D], BF16, "Internal")
    w_in16 = dr("w_in16", [4096, D], BF16, "Internal")
    w_out16 = dr("w_out16", [D, 2048], BF16, "Internal")
    w1_16 = dr("w1_16", [4096, D], BF16, "Internal")
    w2_16 = dr("w2_16", [8, D, 2048], BF16, "Internal")
    wg16 = dr("wg16", [2, D], BF16, "Internal")
    we16 = dr("we16", [8, D], BF16, "Internal")
    wk16 = dr("wk16", [D, D], BF16, "Internal")
    wv16 = dr("wv16", [D, D], BF16, "Internal")
    wr16 = dr("wr16", [D, D], BF16, "Internal")
    wo16 = dr("wo16", [D, D], BF16, "Internal")
    hT_hbm = dr("hT_hbm", [16, 128, T], BF16, "Internal")
    acc_hbm = dr("acc_hbm", [T, D], F32, "Internal")
    xnT_hbm = dr("xnT_hbm", [8, 128, T], BF16, "Internal")
    w16cat = dr("w16cat", [65536, 2048], BF16, "Internal")
    wq16h = dr("wq16h", [D, 8192], BF16, "Internal")

    with ExitStack() as stack:
        iota_sb = stack.enter_context(nc.sbuf_tensor("iotas", [128, 256], F32, allow_name_mangling=True))
        ident_sb = stack.enter_context(nc.sbuf_tensor("idents", [128, 128], BF16, allow_name_mangling=True))
        identf_sb = stack.enter_context(nc.sbuf_tensor("identfs", [128, 128], F32, allow_name_mangling=True))
        eps_col = stack.enter_context(nc.sbuf_tensor("epsc", [128, 1], F32, allow_name_mangling=True))
        ones1_sb = stack.enter_context(nc.sbuf_tensor("ones1", [1, 128], BF16, allow_name_mangling=True))
        castbuf = stack.enter_context(nc.sbuf_tensor("castbuf", [128, 8192], BF16, allow_name_mangling=True))
        block = stack.enter_context(nc.Block())
        c = Ctx(nc, block, stack)

        @block.sync
        def _(s):
            c.inc("cst", s.dma_start(out=iota_sb[:], in_=iota[:, :]), 16)
            c.inc("cst", s.dma_start(out=ident_sb[:], in_=ident[:, :]), 16)
            c.inc("cst", s.dma_start(out=identf_sb[:], in_=identf[:, :]), 16)
            s.wait_ge(c.sem("cst"), c.val("cst"))

        @block.vector
        def _(v):
            v.memset(eps_col[:], LN_EPS)

        kg = emit_keys_cast(c, keys, keys16, castbuf)
        wqg = emit_cast16(c, "w", [(w_q, wq16h)], castbuf)[0]
        cg = emit_cat_cast(c, w_down, w_up, w16cat, castbuf)
        emit_cast16(c, "w", [
            (wq0, wq16a), (wk0, wk16a), (wv0, wv16a),
            (w_in, w_in16), (w_out, w_out16), (w1, w1_16), (w2, w2_16),
            (wg, wg16), (we, we16),
            (wk1, wk16), (wv1, wv16), (wr1, wr16), (wo1, wo16)], castbuf)

        emit_l0f(c, "z", x, xb, xn_hbm, wq16h, wqg, keys16, w16cat, cg, kg,
                 iota_sb, ident_sb, eps_col)
        emit_l0a(c, "z", xb, xa, wq16a, wk16a, wv16a, masks, ident_sb, eps_col)
        emit_l0f(c, "z", xa, xb, xn_hbm, wq16h, wqg, keys16, w16cat, cg, kg,
                 iota_sb, ident_sb, eps_col)
        emit_l1f(c, "z", xb, xa, w_in16, w_out16, w1_16, w2_16, wg16, we16,
                 gb, eb, hT_hbm, acc_hbm, xnT_hbm, ident_sb, eps_col, ones1_sb)
        emit_l1a(c, "z", xa, xb, wk16, wv16, wr16, wo16, tmix, tdecay,
                 tfirst, lng, lnb, ident_sb, eps_col, identf_sb)
        emit_l1f(c, "z", xb, xout, w_in16, w_out16, w1_16, w2_16, wg16, we16,
                 gb, eb, hT_hbm, acc_hbm, xnT_hbm, ident_sb, eps_col, ones1_sb)
    return nc


def make_fused_inmaps(inputs):
    B = 8
    iota, ident = make_consts()
    identf = np.eye(128, dtype=np.float32)
    masks = make_attn_masks()
    x = np.ascontiguousarray(inputs["L0_attn_hidden_states"])
    tmix = np.ascontiguousarray(np.stack([
        inputs["L1_attn_time_mix_k"].reshape(-1),
        inputs["L1_attn_time_mix_v"].reshape(-1),
        inputs["L1_attn_time_mix_r"].reshape(-1)]))
    base = {
        "w_q": inputs["L0_ffn_w_q"], "keys": inputs["L0_ffn_keys"],
        "w_down": inputs["L0_ffn_w_down"], "w_up": inputs["L0_ffn_w_up"],
        "wq0": inputs["L0_attn_Wq"], "wk0": inputs["L0_attn_Wk"],
        "wv0": inputs["L0_attn_Wv"], "masks": masks,
        "w_in": inputs["L1_ffn_w_in"], "w_out": inputs["L1_ffn_w_out"],
        "w1": inputs["L1_ffn_w1"], "w2": inputs["L1_ffn_w2"],
        "wg": inputs["L1_ffn_wg"], "we": inputs["L1_ffn_we"],
        "gb": inputs["L1_ffn_gb"], "eb": inputs["L1_ffn_eb"],
        "wk1": inputs["L1_attn_Wk"], "wv1": inputs["L1_attn_Wv"],
        "wr1": inputs["L1_attn_Wr"], "wo1": inputs["L1_attn_Wo"],
        "tmix": tmix,
        "tdecay": np.ascontiguousarray(inputs["L1_attn_time_decay"].reshape(-1)),
        "tfirst": np.ascontiguousarray(inputs["L1_attn_time_first"].reshape(-1)),
        "lng": inputs["L1_attn_ln_g"], "lnb": inputs["L1_attn_ln_b"],
        "iota": iota, "ident": ident, "identf": identf,
    }
    in_maps = []
    for ci in range(B):
        m = dict(base)
        m["x"] = x[ci]
        in_maps.append(m)
    return in_maps


def kernel_fused(**inputs):
    if "fused" not in _PROGS:
        _PROGS["fused"] = build_fused_program()
    nc = _PROGS["fused"]
    in_maps = make_fused_inmaps(inputs)
    res = run_bass_kernel_spmd(nc, in_maps, core_ids=list(range(8)))
    return np.stack([res.results[ci]["xout"] for ci in range(8)]).astype(np.float32)


_PROGS = {}


def _prog(name):
    if name not in _PROGS:
        _PROGS[name] = {"l0f": build_l0f_program, "l0a": build_l0a_program,
                        "l1f": build_l1f_program, "l1a": build_l1a_program}[name]()
    return _PROGS[name]


def kernel(**inputs):
    if os.environ.get("UNFUSED", "0") != "1":
        return kernel_fused(**inputs)
    """Unfused v1: six launches (L0f, L0a, L0f, L1f, L1a, L1f), one batch
    element per core, activations handed over through host numpy (layout
    only)."""
    B = 8
    iota, ident = make_consts()
    identf = np.eye(128, dtype=np.float32)
    masks = make_attn_masks()
    x = np.ascontiguousarray(inputs["L0_attn_hidden_states"])

    def run(name, base, xs):
        nc = _prog(name)
        in_maps = []
        for ci in range(B):
            m = dict(base)
            m["x"] = xs[ci]
            in_maps.append(m)
        res = run_bass_kernel_spmd(nc, in_maps, core_ids=list(range(B)))
        return [res.results[ci]["xout"] for ci in range(B)]

    l0f_base = {"w_q": inputs["L0_ffn_w_q"], "keys": inputs["L0_ffn_keys"],
                "w_down": inputs["L0_ffn_w_down"], "w_up": inputs["L0_ffn_w_up"],
                "iota": iota, "ident": ident}
    l0a_base = {"wq": inputs["L0_attn_Wq"], "wk": inputs["L0_attn_Wk"],
                "wv": inputs["L0_attn_Wv"], "masks": masks, "ident": ident}
    l1f_base = {"w_in": inputs["L1_ffn_w_in"], "w_out": inputs["L1_ffn_w_out"],
                "w1": inputs["L1_ffn_w1"], "w2": inputs["L1_ffn_w2"],
                "wg": inputs["L1_ffn_wg"], "we": inputs["L1_ffn_we"],
                "gb": inputs["L1_ffn_gb"], "eb": inputs["L1_ffn_eb"],
                "ident": ident}
    tmix = np.ascontiguousarray(np.stack([
        inputs["L1_attn_time_mix_k"].reshape(-1),
        inputs["L1_attn_time_mix_v"].reshape(-1),
        inputs["L1_attn_time_mix_r"].reshape(-1)]))
    l1a_base = {"wk": inputs["L1_attn_Wk"], "wv": inputs["L1_attn_Wv"],
                "wr": inputs["L1_attn_Wr"], "wo": inputs["L1_attn_Wo"],
                "tmix": tmix,
                "tdecay": np.ascontiguousarray(inputs["L1_attn_time_decay"].reshape(-1)),
                "tfirst": np.ascontiguousarray(inputs["L1_attn_time_first"].reshape(-1)),
                "lng": inputs["L1_attn_ln_g"], "lnb": inputs["L1_attn_ln_b"],
                "ident": ident, "identf": identf}

    xs = [x[ci] for ci in range(B)]
    xs = run("l0f", l0f_base, xs)
    xs = run("l0a", l0a_base, xs)
    xs = run("l0f", l0f_base, xs)
    xs = run("l1f", l1f_base, xs)
    xs = run("l1a", l1a_base, xs)
    xs = run("l1f", l1f_base, xs)
    return np.stack(xs).astype(np.float32)
